# version 04
# baseline (speedup 1.0000x reference)
import contextlib
import numpy as np
import concourse.bass as bass
import concourse.mybir as mybir
from concourse.bass_utils import run_bass_kernel_spmd

F32 = mybir.dt.float32
BF16 = mybir.dt.bfloat16
I32 = mybir.dt.int32
AF = mybir.ActivationFunctionType
ALU = mybir.AluOpType
AX = mybir.AxisListType

D = 1024
NT = 4096
NB = 2
S = 2048
H = 16
HD = 64
FC = 8
TCH = 256
NTC = NT // TCH
EPS = 1e-5


class Sem:
    def __init__(self, h):
        self.h = h
        self.n = 0

    def inc(self, instr, k=1):
        instr.then_inc(self.h, k)
        self.n += k
        return instr

    def wait(self, eng):
        eng.wait_ge(self.h, self.n)


class Ctx:
    def __init__(self, nc, block, exit_stack):
        self.nc = nc
        self.block = block
        self.es = exit_stack
        self._sems = {}

    def sem(self, name):
        if name not in self._sems:
            self._sems[name] = Sem(self.es.enter_context(self.nc.semaphore(name)))
        return self._sems[name]

    def sbuf(self, name, shape, dtype, scope=None):
        return (scope or self.es).enter_context(self.nc.sbuf_tensor(name, shape, dtype))

    def barrier(self, engines=("tensor", "vector", "scalar", "gpsimd", "sync")):
        sems = list(self._sems.values())
        for en in engines:
            def mk(eng, sems=sems):
                for s in sems:
                    s.wait(eng)
            getattr(self.block, en)(mk)



def emit_ln(ctx, XR, LNB, XC, consts, tc, sems, xstride=None):
    nc, block = ctx.nc, ctx.block
    s_pe, s_v, s_a = sems
    onesf, ones2, ones_row, ps_sum, ps_mb, ps_rb, row1, rowr, rowv = consts
    NTs = NT if xstride is None else xstride
    c0 = tc * TCH

    @block.tensor
    def _(t):
        s_v.wait(t)
        s_a.wait(t)
        last = None
        for kc in range(FC):
            last = t.matmul(ps_sum[:, :], onesf[:, 0:1], XR[:, kc * NTs + c0:kc * NTs + c0 + TCH],
                            start=(kc == 0), stop=(kc == FC - 1))
        s_pe.inc(last)

    @block.scalar
    def _(sc):
        s_pe.wait(sc)
        s_a.inc(sc.copy(out=row1[:, :], in_=ps_sum[0:1, :]))

    @block.tensor
    def _(t):
        s_a.wait(t)
        s_pe.inc(t.matmul(ps_mb[:, :], ones_row[:, :], row1[:, :], start=True, stop=True))

    @block.vector
    def _(v):
        s_pe.wait(v)
        last = None
        for kc in range(FC):
            last = v.tensor_add(XC[:, kc * TCH:(kc + 1) * TCH],
                                XR[:, kc * NTs + c0:kc * NTs + c0 + TCH], ps_mb[:, :])
        s_v.inc(last)

    @block.scalar
    def _(sc):
        s_v.wait(sc)
        last = None
        for kc in range(FC):
            last = sc.activation(out=LNB[:, kc * TCH:(kc + 1) * TCH],
                                 in_=XC[:, kc * TCH:(kc + 1) * TCH], func=AF.Square)
        s_a.inc(last)

    @block.tensor
    def _(t):
        s_a.wait(t)
        last = None
        for kc in range(FC):
            last = t.matmul(ps_sum[:, :], ones2[:, 1:2], LNB[:, kc * TCH:(kc + 1) * TCH],
                            start=(kc == 0), stop=(kc == FC - 1))
        s_pe.inc(last)

    @block.vector
    def _(v):
        s_pe.wait(v)
        s_v.inc(v.tensor_scalar(rowv[:, :], ps_sum[0:1, :], 1.0 / D, EPS, ALU.mult, ALU.add))

    @block.scalar
    def _(sc):
        s_v.wait(sc)
        s_a.inc(sc.activation(out=rowv[:, :], in_=rowv[:, :], func=AF.Ln))
        s_a.wait(sc)
        s_a.inc(sc.activation(out=rowr[:, :], in_=rowv[:, :], func=AF.Exp, scale=-0.5))

    @block.tensor
    def _(t):
        s_a.wait(t)
        s_pe.inc(t.matmul(ps_rb[:, :], ones_row[:, :], rowr[:, :], start=True, stop=True))

    @block.vector
    def _(v):
        s_pe.wait(v)
        last = None
        for kc in range(FC):
            last = v.tensor_mul(LNB[:, kc * TCH:(kc + 1) * TCH],
                                XC[:, kc * TCH:(kc + 1) * TCH], ps_rb[:, :])
        s_v.inc(last)


def emit_l0_attn(ctx, XR, wq_d, bq_d, wo_d, bo_d, qkT_d, vtok_d, tag="a0",
                 alloc_scope=None):
    nc, block = ctx.nc, ctx.block
    osc = alloc_scope
    s_ld = ctx.sem(tag + "ld")
    s_sp = ctx.sem(tag + "sp")
    s_at = ctx.sem(tag + "at")
    s_pe = ctx.sem(tag + "pe")
    s_v = ctx.sem(tag + "vv")
    s_a = ctx.sem(tag + "aa")
    s_g = ctx.sem(tag + "gg")

    ones2 = ctx.sbuf(tag + "ones2", [128, 2], BF16, scope=osc)
    onesf = ctx.sbuf(tag + "onesf", [128, 1], F32, scope=osc)
    ones_row = ctx.sbuf(tag + "onesr", [1, 128], BF16, scope=osc)
    bq_sb = ctx.sbuf(tag + "bq", [128, 16], F32, scope=osc)
    bv_sb = ctx.sbuf(tag + "bv", [128, 8], BF16, scope=osc)
    bo_sb = ctx.sbuf(tag + "bo", [128, 8], F32, scope=osc)

    with contextlib.ExitStack() as sc1:
        WQ = ctx.sbuf(tag + "wq", [128, FC * 3072], BF16, scope=sc1)
        XC = ctx.sbuf(tag + "xcen", [128, FC * TCH], F32, scope=sc1)
        LNB = ctx.sbuf(tag + "lnb", [128, FC * TCH], BF16, scope=sc1)
        STG = ctx.sbuf(tag + "stg", [128, 8 * TCH], BF16, scope=sc1)
        VSTG = ctx.sbuf(tag + "vstg", [128, 2 * 1024], BF16, scope=sc1)
        row1 = ctx.sbuf(tag + "row1", [1, TCH], BF16, scope=sc1)
        rowr = ctx.sbuf(tag + "rowr", [1, TCH], BF16, scope=sc1)
        rowv = ctx.sbuf(tag + "rowv", [1, TCH], F32, scope=sc1)

        ps_sum = sc1.enter_context(nc.psum_tensor(tag + "ps_sum", [1, TCH], F32))
        ps_mb = sc1.enter_context(nc.psum_tensor(tag + "ps_mb", [128, TCH], F32))
        ps_rb = sc1.enter_context(nc.psum_tensor(tag + "ps_rb", [128, TCH], F32))
        ps_q0 = sc1.enter_context(nc.psum_tensor(tag + "ps_q0", [128, TCH], F32))
        ps_q1 = sc1.enter_context(nc.psum_tensor(tag + "ps_q1", [128, TCH], F32))
        ps_v0 = sc1.enter_context(nc.psum_tensor(tag + "ps_v0", [128, 512], F32))
        ps_v1 = sc1.enter_context(nc.psum_tensor(tag + "ps_v1", [128, 512], F32))

        consts = (onesf, ones2, ones_row, ps_sum, ps_mb, ps_rb, row1, rowr, rowv)

        @block.gpsimd
        def _(g):
            s_ld.inc(g.dma_start(
                out=WQ[:, :].rearrange("p (kc m) -> p kc m", kc=FC),
                in_=wq_d.rearrange("(kc p) m -> p kc m", p=128)), 16)
            s_ld.inc(g.dma_start(out=bq_sb[:, :], in_=bq_d[:, 0:16]), 16)
            s_ld.inc(g.dma_start(out=bv_sb[:, :], in_=bq_d[:, 16:24]), 16)
            s_ld.inc(g.dma_start(out=bo_sb[:, :], in_=bo_d[:, :]), 16)
            g.memset(ones2[:, 0:1], -1.0 / D)
            g.memset(ones2[:, 1:2], 1.0)
            g.memset(onesf[:, 0:1], -1.0 / D)
            s_g.inc(g.memset(ones_row[:, :], 1.0))

        @block.tensor
        def _(t):
            s_g.wait(t)
            s_ld.wait(t)

        for tc in range(NTC):
            c0 = tc * TCH
            emit_ln(ctx, XR, LNB, XC, consts, tc, (s_pe, s_v, s_a))

            for grp in range(2):
                base_v = s_v.n
                base_pe = s_pe.n

                @block.tensor
                def _(t, grp=grp, base_v=base_v):
                    s_v.wait(t)
                    s_sp.wait(t)
                    for j in range(8):
                        ms = grp * 8 + j
                        ps = ps_q0 if j % 2 == 0 else ps_q1
                        if j >= 2:
                            t.wait_ge(s_v.h, base_v + j - 1)
                        last = None
                        for kc in range(FC):
                            last = t.matmul(ps[:, :],
                                            WQ[:, kc * 3072 + ms * 128:kc * 3072 + (ms + 1) * 128],
                                            LNB[:, kc * TCH:(kc + 1) * TCH],
                                            start=(kc == 0), stop=(kc == FC - 1))
                        s_pe.inc(last)

                @block.vector
                def _(v, grp=grp, base_pe=base_pe):
                    for j in range(8):
                        ms = grp * 8 + j
                        ps = ps_q0 if j % 2 == 0 else ps_q1
                        v.wait_ge(s_pe.h, base_pe + j + 1)
                        s_v.inc(v.tensor_scalar(STG[:, j * TCH:(j + 1) * TCH], ps[:, :],
                                                bq_sb[:, ms:ms + 1], None, ALU.add))

                @block.sync
                def _(sy, c0=c0, grp=grp):
                    s_v.wait(sy)
                    s_sp.inc(sy.dma_start(
                        out=qkT_d.rearrange("(ms p) n -> p ms n", p=128)[:, grp * 8:(grp + 1) * 8,
                                                                        c0:c0 + TCH],
                        in_=STG[:, :].rearrange("p (ms n) -> p ms n", ms=8)), 16)

            base_pe = s_pe.n
            base_v = s_v.n

            @block.tensor
            def _(t, base_v=base_v):
                s_sp.wait(t)
                for i in range(4):
                    ts, half = divmod(i, 2)
                    ps = ps_v0 if i % 2 == 0 else ps_v1
                    if i >= 2:
                        t.wait_ge(s_v.h, base_v + i - 1)
                    last = None
                    for kc in range(FC):
                        last = t.matmul(
                            ps[:, :],
                            LNB[:, kc * TCH + ts * 128:kc * TCH + (ts + 1) * 128],
                            WQ[:, kc * 3072 + 2048 + half * 512:kc * 3072 + 2048 + (half + 1) * 512],
                            start=(kc == 0), stop=(kc == FC - 1))
                    s_pe.inc(last)

            @block.vector
            def _(v, base_pe=base_pe):
                for i in range(4):
                    ts, half = divmod(i, 2)
                    ps = ps_v0 if i % 2 == 0 else ps_v1
                    v.wait_ge(s_pe.h, base_pe + i + 1)
                    s_v.inc(v.tensor_copy(
                        VSTG[:, ts * 1024 + half * 512:ts * 1024 + (half + 1) * 512], ps[:, :]))

            @block.scalar
            def _(sc, c0=c0):
                s_v.wait(sc)
                s_sp.inc(sc.dma_start(
                    out=vtok_d[c0:c0 + TCH, :].rearrange("(ts p) n -> p ts n", p=128),
                    in_=VSTG[:, :].rearrange("p (ts n) -> p ts n", ts=2)), 16)

        ctx.barrier()

    QG = 1024

    with contextlib.ExitStack() as sc2:
        WO = ctx.sbuf(tag + "wo", [128, FC * 1024], BF16, scope=sc2)
        qa = [ctx.sbuf(tag + "qa%d" % i, [65, QG], BF16, scope=sc2) for i in range(2)]
        ka = [ctx.sbuf(tag + "ka%d" % i, [65, S], BF16, scope=sc2) for i in range(2)]
        va = [ctx.sbuf(tag + "va%d" % i, [128, 16 * 65], BF16, scope=sc2) for i in range(2)]
        ex = [ctx.sbuf(tag + "ex%d" % i, [128, QG], BF16, scope=sc2) for i in range(2)]
        rb_sb = ctx.sbuf(tag + "rb", [64, QG], BF16, scope=sc2)
        rec_row = ctx.sbuf(tag + "rec", [1, QG], BF16, scope=sc2)
        outn = ctx.sbuf(tag + "outn", [128, FC * QG], BF16, scope=sc2)
        nsq = ctx.sbuf(tag + "nsq", [64, S], BF16, scope=sc2)
        knrow = ctx.sbuf(tag + "knrow", [1, S], BF16, scope=sc2)
        knm2 = ctx.sbuf(tag + "knm2", [1, 1], F32, scope=sc2)
        b64 = ctx.sbuf(tag + "b64", [1, QG], BF16, scope=sc2)

        ps_s = [sc2.enter_context(nc.psum_tensor(tag + "ps_s%d" % i, [128, QG], F32))
                for i in range(2)]
        ps_av = sc2.enter_context(nc.psum_tensor(tag + "ps_av", [65, QG], F32))
        ps_rbb = sc2.enter_context(nc.psum_tensor(tag + "ps_rbb", [64, QG], F32))
        ps_bo = ps_s[0]

        @block.gpsimd
        def _(g):
            s_ld.inc(g.dma_start(
                out=WO[:, :].rearrange("p (kc m) -> p kc m", kc=FC),
                in_=wo_d.rearrange("(kc p) m -> p kc m", p=128)), 16)
            for i in range(2):
                g.memset(va[i][:, :], 1.0)
                s_g.inc(g.memset(ka[i][64:65, :], 1.0))

        for ms in range(8):
            @block.tensor
            def _(t, ms=ms):
                if ms == 0:
                    s_ld.wait(t)
                else:
                    s_v.wait(t)
                last = None
                for kc in range(FC):
                    last = t.matmul(ps_bo[:, 0:1],
                                    WO[:, kc * 1024 + ms * 128:kc * 1024 + (ms + 1) * 128],
                                    bv_sb[:, kc:kc + 1],
                                    start=(kc == 0), stop=(kc == FC - 1))
                s_pe.inc(last)

            @block.vector
            def _(v, ms=ms):
                s_pe.wait(v)
                s_v.inc(v.tensor_add(bo_sb[:, ms:ms + 1], bo_sb[:, ms:ms + 1], ps_bo[:, 0:1]))

        for b in range(NB):
            for qg in range(S // QG):
                for h in range(H):
                    ib = h % 2

                    @block.sync
                    def _(sy, h=h, ib=ib, b=b, qg=qg):
                        s_pe.wait(sy)
                        s_sp.wait(sy)
                        s_g.wait(sy)
                        s_at.inc(sy.dma_start(
                            out=qa[ib][0:64, :],
                            in_=qkT_d[h * 64:(h + 1) * 64,
                                      b * S + qg * QG:b * S + (qg + 1) * QG]), 16)
                        s_at.inc(sy.dma_start(
                            out=ka[ib][0:64, :],
                            in_=qkT_d[1024 + h * 64:1024 + (h + 1) * 64, b * S:(b + 1) * S]), 16)
                        s_at.inc(sy.dma_start(
                            out=va[ib][:, :].rearrange("p (kc d) -> p kc d", kc=16)[:, :, 0:64],
                            in_=vtok_d[b * S:(b + 1) * S, h * 64:(h + 1) * 64]
                            .rearrange("(kc p) d -> p kc d", p=128)), 16)

                    @block.scalar
                    def _(sc, ib=ib):
                        s_at.wait(sc)
                        s_v.wait(sc)
                        s_a.inc(sc.activation(out=nsq[:, :], in_=ka[ib][0:64, :],
                                              func=AF.Square))

                    for g2 in range(2):
                        @block.tensor
                        def _(t, g2=g2):
                            s_a.wait(t)
                            s_v.wait(t)
                            last = None
                            for half in range(2):
                                last = t.matmul(
                                    ps_av[0:1, half * 512:(half + 1) * 512],
                                    ones2[0:64, 1:2],
                                    nsq[:, g2 * 1024 + half * 512:g2 * 1024 + (half + 1) * 512],
                                    start=True, stop=True)
                            s_pe.inc(last)

                        @block.vector
                        def _(v, g2=g2):
                            s_pe.wait(v)
                            s_v.inc(v.tensor_copy(knrow[:, g2 * 1024:(g2 + 1) * 1024],
                                                  ps_av[0:1, 0:1024]))

                    @block.vector
                    def _(v):
                        s_v.wait(v)
                        s_v.inc(v.reduce_max(knm2[:, 0:1], knrow[:, :], axis=AX.X))

                    @block.scalar
                    def _(sc, ib=ib):
                        s_v.wait(sc)
                        s_a.inc(sc.activation(out=nsq[0:64, 0:QG], in_=qa[ib][0:64, :],
                                              func=AF.Square))

                    @block.tensor
                    def _(t):
                        s_a.wait(t)
                        s_v.wait(t)
                        last = None
                        for half in range(2):
                            last = t.matmul(ps_av[0:1, half * 512:(half + 1) * 512],
                                            ones2[0:64, 1:2],
                                            nsq[:, half * 512:(half + 1) * 512],
                                            start=True, stop=True)
                        s_pe.inc(last)

                    @block.vector
                    def _(v):
                        s_pe.wait(v)
                        s_v.inc(v.tensor_scalar(b64[0:1, :], ps_av[0:1, :], knm2[0:1, 0:1],
                                                -0.5, ALU.add, ALU.mult))

                    @block.gpsimd
                    def _(g, ib=ib):
                        s_v.wait(g)
                        s_at.inc(g.dma_start(out=qa[ib][64:65, :], in_=b64[0:1, :]), 16)

                    for kc in range(16):
                        @block.tensor
                        def _(t, h=h, kc=kc, ib=ib, qg=qg):
                            if kc == 0:
                                s_v.wait(t)
                                s_at.wait(t)
                            s_a.wait(t)
                            last = None
                            for half in range(2):
                                last = t.matmul(
                                    ps_s[kc % 2][:, half * 512:(half + 1) * 512],
                                    ka[ib][:, kc * 128:(kc + 1) * 128],
                                    qa[ib][:, half * 512:(half + 1) * 512],
                                    start=True, stop=True)
                            s_pe.inc(last)

                        @block.scalar
                        def _(sc, kc=kc):
                            s_pe.wait(sc)
                            s_a.inc(sc.activation(out=ex[kc % 2][:, :], in_=ps_s[kc % 2][:, :],
                                                  func=AF.Exp, scale=0.125))

                        @block.tensor
                        def _(t, kc=kc, ib=ib):
                            s_a.wait(t)
                            last = None
                            for half in range(2):
                                last = t.matmul(
                                    ps_av[:, half * 512:(half + 1) * 512],
                                    va[ib][:, kc * 65:(kc + 1) * 65],
                                    ex[kc % 2][:, half * 512:(half + 1) * 512],
                                    start=(kc == 0), stop=(kc == 15))
                            s_pe.inc(last)

                    @block.vector
                    def _(v):
                        s_pe.wait(v)
                        with nc.allow_low_precision(reason="bf16 matmul operand"):
                            s_v.inc(v.reciprocal(rec_row[:, :], ps_av[64:65, :]))

                    @block.tensor
                    def _(t):
                        s_v.wait(t)
                        t.matmul(ps_rbb[:, 0:512], ones_row[:, 0:64], rec_row[:, 0:512],
                                 start=True, stop=True)
                        s_pe.inc(t.matmul(ps_rbb[:, 512:1024], ones_row[:, 0:64],
                                          rec_row[:, 512:1024], start=True, stop=True))

                    @block.scalar
                    def _(sc):
                        s_pe.wait(sc)
                        s_a.inc(sc.copy(out=rb_sb[:, :], in_=ps_rbb[:, :]))

                    @block.vector
                    def _(v, h=h):
                        s_a.wait(v)
                        ch = h // 2
                        rowbase = (h % 2) * 64
                        s_v.inc(v.tensor_mul(outn[rowbase:rowbase + 64, ch * QG:(ch + 1) * QG],
                                             ps_av[0:64, :], rb_sb[:, :]))

                base_v = s_v.n

                @block.tensor
                def _(t, base_v=base_v):
                    s_v.wait(t)
                    s_a.wait(t)
                    for ms in range(8):
                        ps = ps_s[ms % 2]
                        if ms >= 2:
                            t.wait_ge(s_v.h, base_v + ms - 1)
                        last = None
                        for half in range(2):
                            for kc in range(FC):
                                last = t.matmul(
                                    ps[:, half * 512:(half + 1) * 512],
                                    WO[:, kc * 1024 + ms * 128:kc * 1024 + (ms + 1) * 128],
                                    outn[:, kc * QG + half * 512:kc * QG + (half + 1) * 512],
                                    start=(kc == 0), stop=(kc == FC - 1))
                        s_pe.inc(last)

                base_pe = s_pe.n - 8

                @block.vector
                def _(v, b=b, qg=qg, base_pe=base_pe):
                    for ms in range(8):
                        ps = ps_s[ms % 2]
                        v.wait_ge(s_pe.h, base_pe + ms + 1)
                        last = None
                        for half in range(2):
                            col = b * S + qg * QG + half * 512
                            sl = XR[:, ms * NT + col:ms * NT + col + 512]
                            last = v.scalar_tensor_tensor(
                                sl, ps[:, half * 512:(half + 1) * 512],
                                bo_sb[:, ms:ms + 1], sl, ALU.add, ALU.add)
                        s_v.inc(last)

                ctx.barrier()




def _prov_flat(h):
    def xprov(c0, w):
        return bass.AP(h, c0, [[4096, 128], [128 * 4096, 8], [1, w]])

    def xrow(f0, c0, w):
        return bass.AP(h, f0 * 4096 + c0, [[4096, 128], [256, w // 256], [1, 256]])

    def yprov(c0, w):
        return bass.AP(h, c0, [[4096, 128], [128 * 4096, 8], [1, w]])
    return xprov, xrow, yprov


def _prov_a2_ss(h):
    def xprov(c0, w):
        assert w == 256 and c0 % 256 == 0
        t = c0 // 256
        off = (t // 2) * 1024 * 512 + (t % 2) * 256
        return bass.AP(h, off, [[512, 128], [128 * 512, 8], [1, 256]])
    return xprov, None, xprov


def _prov_a2_bs(h):
    def xprov(c0, w):
        assert w == 256 and c0 % 256 == 0
        tb = c0 // 256
        b, j = tb // 8, tb % 8
        off = j * 1024 * 512 + b * 256
        return bass.AP(h, off, [[512, 128], [128 * 512, 8], [1, 256]])

    def xrow(f0, c0, w):
        tb = c0 // 256
        b, j0 = tb // 8, tb % 8
        k = w // 256
        assert j0 + k <= 8
        off = (j0 * 1024 + f0) * 512 + b * 256
        return bass.AP(h, off, [[512, 128], [1024 * 512, k], [1, 256]])
    return xprov, xrow, xprov



TS_FF = 256


def emit_l0_ffn(ctx, xprov, yprov, wT_d, uT_d, bcol_d, inc_d, lnx_d, gd_d, tag="f0"):
    nc, block = ctx.nc, ctx.block
    s_ld = ctx.sem(tag + "ld")
    s_sp = ctx.sem(tag + "sp")
    s_pe = ctx.sem(tag + "pe")
    s_v = ctx.sem(tag + "vv")
    s_a = ctx.sem(tag + "aa")
    s_g = ctx.sem(tag + "gg")

    ones2 = ctx.sbuf(tag + "ones2", [128, 2], BF16)
    onesf = ctx.sbuf(tag + "onesf", [128, 1], F32)
    ones_row = ctx.sbuf(tag + "onesr", [1, 128], BF16)
    bcol = ctx.sbuf(tag + "bcol", [128, 32], F32)
    incc = ctx.sbuf(tag + "incc", [128, 8], F32)

    @block.gpsimd
    def _(g):
        s_ld.inc(g.dma_start(out=bcol[:, :], in_=bcol_d[:, :]), 16)
        s_ld.inc(g.dma_start(out=incc[:, :], in_=inc_d[:, :]), 16)
        g.memset(ones2[:, 0:1], -1.0 / D)
        g.memset(ones2[:, 1:2], 1.0)
        g.memset(onesf[:, 0:1], -1.0 / D)
        s_g.inc(g.memset(ones_row[:, :], 1.0))

    with contextlib.ExitStack() as sca:
        XRC = ctx.sbuf(tag + "xrc", [128, FC * TCH], F32, scope=sca)
        XC = ctx.sbuf(tag + "xc", [128, FC * TCH], F32, scope=sca)
        LNB = ctx.sbuf(tag + "lnb", [128, FC * TCH], BF16, scope=sca)
        row1 = ctx.sbuf(tag + "row1", [1, TCH], BF16, scope=sca)
        rowr = ctx.sbuf(tag + "rowr", [1, TCH], BF16, scope=sca)
        rowv = ctx.sbuf(tag + "rowv", [1, TCH], F32, scope=sca)
        ps_sum = sca.enter_context(nc.psum_tensor(tag + "ps_sum", [1, TCH], F32))
        ps_mb = sca.enter_context(nc.psum_tensor(tag + "ps_mb", [128, TCH], F32))
        ps_rb = sca.enter_context(nc.psum_tensor(tag + "ps_rb", [128, TCH], F32))
        consts = (onesf, ones2, ones_row, ps_sum, ps_mb, ps_rb, row1, rowr, rowv)

        for tc in range(NTC):
            c0 = tc * TCH

            @block.sync
            def _(sy, c0=c0):
                s_a.wait(sy)
                s_v.wait(sy)
                s_pe.wait(sy)
                s_sp.wait(sy)
                s_g.inc(sy.dma_start(
                    out=XRC[:, :].rearrange("p (kc n) -> p kc n", kc=FC),
                    in_=xprov(c0, TCH)), 16)

            @block.tensor
            def _(t):
                s_g.wait(t)
                s_ld.wait(t)

            @block.vector
            def _(v):
                s_g.wait(v)

            @block.scalar
            def _(sc):
                s_g.wait(sc)

            emit_ln(ctx, XRC, LNB, XC, consts, 0, (s_pe, s_v, s_a), xstride=TCH)

            @block.sync
            def _(sy, c0=c0):
                s_v.wait(sy)
                s_sp.inc(sy.dma_start(
                    out=lnx_d.rearrange("(kc p) n -> p kc n", p=128)[:, :, c0:c0 + TCH],
                    in_=LNB[:, :].rearrange("p (kc n) -> p kc n", kc=FC)), 16)

        ctx.barrier()

    with contextlib.ExitStack() as scb:
        WG = ctx.sbuf(tag + "wg", [128, 4 * FC * 1024], BF16, scope=scb)
        LNL = ctx.sbuf(tag + "lnl", [128, FC * TCH], BF16, scope=scb)
        STG = ctx.sbuf(tag + "stgb", [128, FC * TCH], F32, scope=scb)
        ps_p = [scb.enter_context(nc.psum_tensor(tag + "ps_p%d" % i, [128, TCH], F32))
                for i in range(2)]

        @block.gpsimd
        def _(g):
            for w in range(4):
                s_ld.inc(g.dma_start(
                    out=WG[:, w * 8192:(w + 1) * 8192].rearrange("p (kc m) -> p kc m", kc=FC),
                    in_=wT_d[w].rearrange("(kc p) m -> p kc m", p=128)), 16)

        for tc in range(NTC):
            c0 = tc * TCH

            @block.sync
            def _(sy, c0=c0):
                s_pe.wait(sy)
                s_ld.wait(sy)
                s_g.inc(sy.dma_start(
                    out=LNL[:, :].rearrange("p (kc n) -> p kc n", kc=FC),
                    in_=lnx_d.rearrange("(kc p) n -> p kc n", p=128)[:, :, c0:c0 + TCH]), 16)

            for w in range(4):
                base_v = s_v.n
                base_pe = s_pe.n

                @block.tensor
                def _(t, w=w, base_v=base_v):
                    s_g.wait(t)
                    s_sp.wait(t)
                    for ms in range(8):
                        ps = ps_p[ms % 2]
                        if ms >= 2:
                            t.wait_ge(s_v.h, base_v + ms - 1)
                        last = None
                        for kc in range(FC):
                            last = t.matmul(
                                ps[:, :],
                                WG[:, w * 8192 + kc * 1024 + ms * 128:w * 8192 + kc * 1024 + (ms + 1) * 128],
                                LNL[:, kc * TCH:(kc + 1) * TCH],
                                start=(kc == 0), stop=(kc == FC - 1))
                        s_pe.inc(last)

                @block.vector
                def _(v, w=w, base_pe=base_pe):
                    for ms in range(8):
                        ps = ps_p[ms % 2]
                        v.wait_ge(s_pe.h, base_pe + ms + 1)
                        s_v.inc(v.tensor_scalar(STG[:, ms * TCH:(ms + 1) * TCH], ps[:, :],
                                                bcol[:, w * 8 + ms:w * 8 + ms + 1], None,
                                                ALU.add))

                @block.sync
                def _(sy, w=w, c0=c0):
                    s_v.wait(sy)
                    s_sp.inc(sy.dma_start(
                        out=gd_d[w].rearrange("(kc p) n -> p kc n", p=128)[:, :, c0:c0 + TCH],
                        in_=STG[:, :].rearrange("p (kc n) -> p kc n", kc=FC)), 16)

        ctx.barrier()

    CC = ctx.sbuf(tag + "cc", [128, FC * TS_FF], F32)
    CB = ctx.sbuf(tag + "cb", [128, FC * TS_FF], BF16)
    with contextlib.ExitStack() as scc:
        U3 = ctx.sbuf(tag + "u3", [128, 3 * FC * 1024], BF16, scope=scc)
        GA = [ctx.sbuf(tag + "ga%d" % w, [128, FC * TS_FF], F32, scope=scc) for w in range(3)]
        ps_p = [scc.enter_context(nc.psum_tensor(tag + "ps_c%d" % i, [128, TS_FF], F32))
                for i in range(2)]

        @block.gpsimd
        def _(g):
            for w in range(3):
                s_ld.inc(g.dma_start(
                    out=U3[:, w * 8192:(w + 1) * 8192].rearrange("p (kc m) -> p kc m", kc=FC),
                    in_=uT_d[w].rearrange("(kc p) m -> p kc m", p=128)), 16)

        @block.vector
        def _(v):
            s_ld.wait(v)
            last = None
            for kc in range(FC):
                last = v.tensor_copy(CC[:, kc * TS_FF:(kc + 1) * TS_FF],
                                     bass.AP(incc, kc, [[8, 128], [0, TS_FF]]))
            s_v.inc(last)
            s_v.wait(v)
            last = None
            for kc in range(FC):
                last = v.tensor_copy(CB[:, kc * TS_FF:(kc + 1) * TS_FF],
                                     CC[:, kc * TS_FF:(kc + 1) * TS_FF])
            s_v.inc(last)

        for t_ in range(16):
            ct = t_ * TS_FF

            @block.sync
            def _(sy, ct=ct):
                s_a.wait(sy)
                s_v.wait(sy)
                for w in range(3):
                    s_g.inc(sy.dma_start(
                        out=GA[w][:, :].rearrange("p (kc n) -> p kc n", kc=FC),
                        in_=gd_d[w].rearrange("(kc p) n -> p kc n", p=128)[:, :, ct:ct + TS_FF]),
                        16)

            for w in range(3):
                base_v = s_v.n
                base_pe = s_pe.n

                @block.tensor
                def _(t, w=w, base_v=base_v):
                    s_v.wait(t)
                    s_g.wait(t)
                    for ms in range(8):
                        ps = ps_p[ms % 2]
                        if ms >= 2:
                            t.wait_ge(s_v.h, base_v + ms - 1)
                        last = None
                        for kc in range(FC):
                            last = t.matmul(
                                ps[:, :],
                                U3[:, w * 8192 + kc * 1024 + ms * 128:w * 8192 + kc * 1024 + (ms + 1) * 128],
                                CB[:, kc * TS_FF:(kc + 1) * TS_FF],
                                start=(kc == 0), stop=(kc == FC - 1))
                        s_pe.inc(last)

                @block.vector
                def _(v, w=w, base_pe=base_pe):
                    for ms in range(8):
                        ps = ps_p[ms % 2]
                        v.wait_ge(s_pe.h, base_pe + ms + 1)
                        sl = GA[w][:, ms * TS_FF:(ms + 1) * TS_FF]
                        s_v.inc(v.scalar_tensor_tensor(sl, ps[:, :], 1.0, sl,
                                                       ALU.mult, ALU.add))

            @block.scalar
            def _(sc):
                s_v.wait(sc)
                sc.activation(out=GA[0][:, :], in_=GA[0][:, :], func=AF.Sigmoid)
                sc.activation(out=GA[1][:, :], in_=GA[1][:, :], func=AF.Sigmoid)
                s_a.inc(sc.activation(out=GA[2][:, :], in_=GA[2][:, :], func=AF.Tanh))

            @block.vector
            def _(v):
                s_a.wait(v)
                last = None
                for kc in range(FC):
                    sl = slice(kc * TS_FF, (kc + 1) * TS_FF)
                    last = v.tensor_mul(CC[:, sl], CC[:, sl], GA[0][:, sl])
                s_v.inc(last)
                s_v.wait(v)
                for kc in range(FC):
                    sl = slice(kc * TS_FF, (kc + 1) * TS_FF)
                    last = v.tensor_mul(GA[1][:, sl], GA[1][:, sl], GA[2][:, sl])
                s_v.inc(last)
                s_v.wait(v)
                for kc in range(FC):
                    sl = slice(kc * TS_FF, (kc + 1) * TS_FF)
                    last = v.tensor_add(CC[:, sl], CC[:, sl], GA[1][:, sl])
                s_v.inc(last)
                s_v.wait(v)
                for kc in range(FC):
                    sl = slice(kc * TS_FF, (kc + 1) * TS_FF)
                    last = v.tensor_copy(CB[:, sl], CC[:, sl])
                s_v.inc(last)

        ctx.barrier()

    with contextlib.ExitStack() as scd:
        UOW = ctx.sbuf(tag + "uow", [128, FC * 1024], BF16, scope=scd)
        UOT = ctx.sbuf(tag + "uot", [128, FC * TS_FF], F32, scope=scd)
        HOC = ctx.sbuf(tag + "hoc", [128, FC * TS_FF], F32, scope=scd)
        XRC2 = ctx.sbuf(tag + "xrc2", [128, FC * TS_FF], F32, scope=scd)
        ps_p = [scd.enter_context(nc.psum_tensor(tag + "ps_d%d" % i, [128, TS_FF], F32))
                for i in range(2)]

        @block.gpsimd
        def _(g):
            s_ld.inc(g.dma_start(
                out=UOW[:, :].rearrange("p (kc m) -> p kc m", kc=FC),
                in_=uT_d[3].rearrange("(kc p) m -> p kc m", p=128)), 16)

        base_v = s_v.n
        base_pe = s_pe.n

        @block.tensor
        def _(t, base_v=base_v):
            s_v.wait(t)
            s_ld.wait(t)
            for ms in range(8):
                ps = ps_p[ms % 2]
                if ms >= 2:
                    t.wait_ge(s_v.h, base_v + ms - 1)
                last = None
                for kc in range(FC):
                    last = t.matmul(ps[:, :],
                                    UOW[:, kc * 1024 + ms * 128:kc * 1024 + (ms + 1) * 128],
                                    CB[:, kc * TS_FF:(kc + 1) * TS_FF],
                                    start=(kc == 0), stop=(kc == FC - 1))
                s_pe.inc(last)

        @block.vector
        def _(v, base_pe=base_pe):
            for ms in range(8):
                ps = ps_p[ms % 2]
                v.wait_ge(s_pe.h, base_pe + ms + 1)
                s_v.inc(v.tensor_copy(UOT[:, ms * TS_FF:(ms + 1) * TS_FF], ps[:, :]))

        for t_ in range(16):
            ct = t_ * TS_FF

            @block.sync
            def _(sy, ct=ct):
                s_v.wait(sy)
                s_sp.wait(sy)
                s_g.inc(sy.dma_start(
                    out=HOC[:, :].rearrange("p (kc n) -> p kc n", kc=FC),
                    in_=gd_d[3].rearrange("(kc p) n -> p kc n", p=128)[:, :, ct:ct + TS_FF]), 16)
                s_g.inc(sy.dma_start(
                    out=XRC2[:, :].rearrange("p (kc n) -> p kc n", kc=FC),
                    in_=xprov(ct, TS_FF)), 16)

            @block.vector
            def _(v):
                s_g.wait(v)
                last = None
                for kc in range(FC):
                    sl = slice(kc * TS_FF, (kc + 1) * TS_FF)
                    last = v.tensor_add(HOC[:, sl], HOC[:, sl], UOT[:, sl])
                s_v.inc(last)

            @block.scalar
            def _(sc):
                s_v.wait(sc)
                s_a.inc(sc.activation(out=HOC[:, :], in_=HOC[:, :], func=AF.Tanh))

            @block.vector
            def _(v):
                s_a.wait(v)
                last = None
                for kc in range(FC):
                    sl = slice(kc * TS_FF, (kc + 1) * TS_FF)
                    last = v.tensor_add(XRC2[:, sl], XRC2[:, sl], HOC[:, sl])
                s_v.inc(last)

            @block.sync
            def _(sy, ct=ct):
                s_v.wait(sy)
                s_sp.inc(sy.dma_start(
                    out=yprov(ct, TS_FF),
                    in_=XRC2[:, :].rearrange("p (kc n) -> p kc n", kc=FC)), 16)

        ctx.barrier()


def _run_l0_ffn(x_t, wf, uf, bf, wi, ui, bi, wc, uc, bc, wo, uo, bo, in_c):
    nc = bass.Bass("TRN2", target_bir_lowering=False, debug=False)
    xin_h = nc.dram_tensor("xT", [D, NT], F32, kind="ExternalInput")
    xin = xin_h.ap()
    wT, uT = [], []
    for nm in ("wf", "wi", "wc", "wo"):
        wT.append(nc.dram_tensor(nm, [D, D], F32, kind="ExternalInput").ap())
    for nm in ("uf", "ui", "uc", "uo"):
        uT.append(nc.dram_tensor(nm, [D, D], F32, kind="ExternalInput").ap())
    bcol_d = nc.dram_tensor("bcol", [128, 32], F32, kind="ExternalInput").ap()
    inc_d = nc.dram_tensor("incc", [128, 8], F32, kind="ExternalInput").ap()
    yout_h = nc.dram_tensor("yT", [D, NT], F32, kind="ExternalOutput")
    lnx_d = nc.dram_tensor("lnx", [D, NT], BF16).ap()
    gd = [nc.dram_tensor("gd%d" % i, [D, NT], F32).ap() for i in range(4)]

    with contextlib.ExitStack() as es:
        block = es.enter_context(nc.Block())
        ctx = Ctx(nc, block, es)
        xprov = _prov_flat(xin_h)[0]
        yprov = _prov_flat(yout_h)[2]
        emit_l0_ffn(ctx, xprov, yprov, wT, uT, bcol_d, inc_d, lnx_d, gd)

        ctx.barrier(engines=("sync",))

    def tcol(a):
        return np.ascontiguousarray(np.asarray(a, np.float32).T)

    bcol = np.ascontiguousarray(
        np.stack([np.asarray(b, np.float32).reshape(8, 128) for b in (bf, bi, bc, bo)])
        .reshape(32, 128).T)
    incc = np.ascontiguousarray(np.asarray(in_c, np.float32).reshape(8, 128).T)
    in_maps = [{"xT": x_t[i], "wf": tcol(wf), "wi": tcol(wi), "wc": tcol(wc),
                "wo": tcol(wo), "uf": tcol(uf), "ui": tcol(ui), "uc": tcol(uc),
                "uo": tcol(uo), "bcol": bcol, "incc": incc} for i in range(8)]
    res = run_bass_kernel_spmd(nc, in_maps, core_ids=list(range(8)))
    return np.stack([np.asarray(r["yT"]) for r in res.results])



def emit_l1_attn(ctx, xprov, xrow, yout_d, wqT_d, woT_d, qkT_d, vtok_d, tag="b1"):
    nc, block = ctx.nc, ctx.block
    s_ld = ctx.sem(tag + "ld")
    s_sp = ctx.sem(tag + "sp")
    s_at = ctx.sem(tag + "at")
    s_pe = ctx.sem(tag + "pe")
    s_v = ctx.sem(tag + "vv")
    s_a = ctx.sem(tag + "aa")
    s_g = ctx.sem(tag + "gg")
    PI = 3.14159265358979323846

    ones2 = ctx.sbuf(tag + "ones2", [128, 2], BF16)
    onesf = ctx.sbuf(tag + "onesf", [128, 1], F32)
    ones_row = ctx.sbuf(tag + "onesr", [1, 128], BF16)

    @block.gpsimd
    def _(g):
        g.memset(ones2[:, 0:1], -1.0 / D)
        g.memset(ones2[:, 1:2], 1.0)
        g.memset(onesf[:, 0:1], -1.0 / D)
        s_g.inc(g.memset(ones_row[:, :], 1.0))

    with contextlib.ExitStack() as sc1:
        WQ = ctx.sbuf(tag + "wq", [128, FC * 3072], BF16, scope=sc1)
        XRC = ctx.sbuf(tag + "xrc", [128, FC * TCH], F32, scope=sc1)
        XC = ctx.sbuf(tag + "xcen", [128, FC * TCH], F32, scope=sc1)
        LNB = ctx.sbuf(tag + "lnb", [128, FC * TCH], BF16, scope=sc1)
        STG = ctx.sbuf(tag + "stg", [128, 8 * TCH], BF16, scope=sc1)
        VSTG = ctx.sbuf(tag + "vstg", [128, 2 * 1024], BF16, scope=sc1)
        row1 = ctx.sbuf(tag + "row1", [1, TCH], BF16, scope=sc1)
        rowr = ctx.sbuf(tag + "rowr", [1, TCH], BF16, scope=sc1)
        rowv = ctx.sbuf(tag + "rowv", [1, TCH], F32, scope=sc1)

        ps_sum = sc1.enter_context(nc.psum_tensor(tag + "ps_sum", [1, TCH], F32))
        ps_mb = sc1.enter_context(nc.psum_tensor(tag + "ps_mb", [128, TCH], F32))
        ps_rb = sc1.enter_context(nc.psum_tensor(tag + "ps_rb", [128, TCH], F32))
        ps_q0 = sc1.enter_context(nc.psum_tensor(tag + "ps_q0", [128, TCH], F32))
        ps_q1 = sc1.enter_context(nc.psum_tensor(tag + "ps_q1", [128, TCH], F32))
        ps_v0 = sc1.enter_context(nc.psum_tensor(tag + "ps_v0", [128, 512], F32))
        ps_v1 = sc1.enter_context(nc.psum_tensor(tag + "ps_v1", [128, 512], F32))
        consts = (onesf, ones2, ones_row, ps_sum, ps_mb, ps_rb, row1, rowr, rowv)

        @block.gpsimd
        def _(g):
            s_ld.inc(g.dma_start(
                out=WQ[:, :].rearrange("p (kc m) -> p kc m", kc=FC),
                in_=wqT_d.rearrange("(kc p) m -> p kc m", p=128)), 16)

        for tc in range(NTC):
            c0 = tc * TCH

            @block.sync
            def _(sy, c0=c0):
                s_a.wait(sy)
                s_v.wait(sy)
                s_pe.wait(sy)
                s_sp.wait(sy)
                s_g.inc(sy.dma_start(
                    out=XRC[:, :].rearrange("p (kc n) -> p kc n", kc=FC),
                    in_=xprov(c0, TCH)), 16)

            @block.tensor
            def _(t):
                s_g.wait(t)
                s_ld.wait(t)

            @block.vector
            def _(v):
                s_g.wait(v)

            @block.scalar
            def _(sc):
                s_g.wait(sc)

            emit_ln(ctx, XRC, LNB, XC, consts, 0, (s_pe, s_v, s_a), xstride=TCH)

            for grp in range(2):
                base_v = s_v.n
                base_pe = s_pe.n

                @block.tensor
                def _(t, grp=grp, base_v=base_v):
                    s_v.wait(t)
                    s_sp.wait(t)
                    for j in range(8):
                        ms = grp * 8 + j
                        ps = ps_q0 if j % 2 == 0 else ps_q1
                        if j >= 2:
                            t.wait_ge(s_v.h, base_v + j - 1)
                        last = None
                        for kc in range(FC):
                            last = t.matmul(ps[:, :],
                                            WQ[:, kc * 3072 + ms * 128:kc * 3072 + (ms + 1) * 128],
                                            LNB[:, kc * TCH:(kc + 1) * TCH],
                                            start=(kc == 0), stop=(kc == FC - 1))
                        s_pe.inc(last)

                @block.vector
                def _(v, grp=grp, base_pe=base_pe):
                    for j in range(8):
                        ps = ps_q0 if j % 2 == 0 else ps_q1
                        v.wait_ge(s_pe.h, base_pe + j + 1)
                        s_v.inc(v.tensor_copy(STG[:, j * TCH:(j + 1) * TCH], ps[:, :]))

                @block.sync
                def _(sy, c0=c0, grp=grp):
                    s_v.wait(sy)
                    s_sp.inc(sy.dma_start(
                        out=qkT_d.rearrange("(ms p) n -> p ms n", p=128)[:, grp * 8:(grp + 1) * 8,
                                                                        c0:c0 + TCH],
                        in_=STG[:, :].rearrange("p (ms n) -> p ms n", ms=8)), 16)

            base_pe = s_pe.n
            base_v = s_v.n

            @block.tensor
            def _(t, base_v=base_v):
                s_sp.wait(t)
                for i in range(4):
                    ts, half = divmod(i, 2)
                    ps = ps_v0 if i % 2 == 0 else ps_v1
                    if i >= 2:
                        t.wait_ge(s_v.h, base_v + i - 1)
                    last = None
                    for kc in range(FC):
                        last = t.matmul(
                            ps[:, :],
                            LNB[:, kc * TCH + ts * 128:kc * TCH + (ts + 1) * 128],
                            WQ[:, kc * 3072 + 2048 + half * 512:kc * 3072 + 2048 + (half + 1) * 512],
                            start=(kc == 0), stop=(kc == FC - 1))
                    s_pe.inc(last)

            @block.vector
            def _(v, base_pe=base_pe):
                for i in range(4):
                    ts, half = divmod(i, 2)
                    ps = ps_v0 if i % 2 == 0 else ps_v1
                    v.wait_ge(s_pe.h, base_pe + i + 1)
                    s_v.inc(v.tensor_copy(
                        VSTG[:, ts * 1024 + half * 512:ts * 1024 + (half + 1) * 512], ps[:, :]))

            @block.scalar
            def _(sc, c0=c0):
                s_v.wait(sc)
                s_sp.inc(sc.dma_start(
                    out=vtok_d[c0:c0 + TCH, :].rearrange("(ts p) n -> p ts n", p=128),
                    in_=VSTG[:, :].rearrange("p (ts n) -> p ts n", ts=2)), 16)

        ctx.barrier()

    with contextlib.ExitStack() as sc2:
        C1 = ctx.sbuf(tag + "c1", [128, S], F32, scope=sc2)
        C2 = ctx.sbuf(tag + "c2", [128, S], F32, scope=sc2)
        T1 = ctx.sbuf(tag + "t1", [128, S], F32, scope=sc2)
        T2 = ctx.sbuf(tag + "t2", [128, S], F32, scope=sc2)
        QIN = ctx.sbuf(tag + "qin", [128, S], BF16, scope=sc2)
        KFC = ctx.sbuf(tag + "kfc", [128, S], BF16, scope=sc2)
        Pm = ctx.sbuf(tag + "pm", [128, 128], BF16, scope=sc2)
        IDN = ctx.sbuf(tag + "idn", [128, 128], BF16, scope=sc2)
        pi32 = ctx.sbuf(tag + "pi32", [128, 1], I32, scope=sc2)
        sgn = ctx.sbuf(tag + "sgn", [128, 1], F32, scope=sc2)
        tsw = ctx.sbuf(tag + "tsw", [128, 1], F32, scope=sc2)
        invf = ctx.sbuf(tag + "invf", [128, 1], F32, scope=sc2)
        QF = ctx.sbuf(tag + "qf", [128, FC * S], BF16, scope=sc2)
        KT = ctx.sbuf(tag + "kt", [128, 16 * 1024], BF16, scope=sc2)
        va = [ctx.sbuf(tag + "va%d" % i, [128, 16 * 65], BF16, scope=sc2) for i in range(2)]
        kvs = ctx.sbuf(tag + "kvs", [64, 65], BF16, scope=sc2)
        kvs_hi = ctx.sbuf(tag + "kvsh", [128, 65], BF16, scope=sc2)
        outn = ctx.sbuf(tag + "outn", [128, FC * S], BF16, scope=sc2)
        rec_row = ctx.sbuf(tag + "rec", [1, S], BF16, scope=sc2)
        rb_sb = ctx.sbuf(tag + "rb", [64, 1024], BF16, scope=sc2)
        WOUT = ctx.sbuf(tag + "wout", [128, FC * 1024], BF16, scope=sc2)
        tbl_scope = contextlib.ExitStack()
        TH = ctx.sbuf(tag + "th", [128, S], F32, scope=tbl_scope)
        TI = ctx.sbuf(tag + "ti", [128, S], I32, scope=tbl_scope)
        i5 = ctx.sbuf(tag + "i5", [128, 1], I32, scope=tbl_scope)
        ib1 = ctx.sbuf(tag + "ib1", [128, 1], I32, scope=tbl_scope)
        iw = ctx.sbuf(tag + "iw", [128, 1], I32, scope=tbl_scope)
        recf = T1

        @block.gpsimd
        def _(g):
            s_ld.inc(g.dma_start(
                out=WOUT[:, :].rearrange("p (kc m) -> p kc m", kc=FC),
                in_=woT_d.rearrange("(kc p) m -> p kc m", p=128)), 16)
            g.iota(pi32[:, :], [[1, 1]], base=0, channel_multiplier=1)
            g.iota(TH[:, :], [[1, S]], base=0, channel_multiplier=0,
                   allow_small_or_imprecise_dtypes=True)
            g.memset(IDN[:, :], 0.0)
            g.affine_select(out=IDN[:, :], in_=IDN[:, :], compare_op=ALU.not_equal,
                            fill=1.0, base=0, pattern=[[-1, 128]], channel_multiplier=1)
            for i in range(2):
                g.memset(va[i][:, :], 1.0)
            s_g.inc(g.nop())

        @block.vector
        def _(v):
            s_g.wait(v)
            s_v.inc(v.tensor_scalar(i5[:, :], pi32[:, :], 5, None, ALU.arith_shift_right))
            s_v.wait(v)
            s_v.inc(v.tensor_scalar(ib1[:, :], i5[:, :], 1, None, ALU.bitwise_and))
            s_v.wait(v)
            s_v.inc(v.tensor_scalar(iw[:, :], ib1[:, :], -64, 32, ALU.mult, ALU.add))
            s_v.wait(v)
            s_v.inc(v.tensor_tensor(iw[:, :], iw[:, :], pi32[:, :], ALU.add))
            s_v.wait(v)
            s_v.inc(v.tensor_copy(tsw[:, :], iw[:, :]))
            s_v.wait(v)
            s_v.inc(v.tensor_scalar(sgn[:, :], ib1[:, :], 2, -1, ALU.mult, ALU.add))
            s_v.wait(v)
            s_v.inc(v.tensor_scalar(Pm[:, :], TH[:, 0:128], tsw[:, 0:1], None,
                                    ALU.is_equal))
            s_v.wait(v)
            s_v.inc(v.tensor_scalar(iw[:, :], i5[:, :], -32, None, ALU.mult))
            s_v.wait(v)
            s_v.inc(v.tensor_tensor(iw[:, :], iw[:, :], pi32[:, :], ALU.add))
            s_v.wait(v)
            s_v.inc(v.tensor_copy(invf[:, :], iw[:, :]))

        @block.scalar
        def _(sc):
            s_v.wait(sc)
            import math as _m
            s_a.inc(sc.activation(out=invf[:, :], in_=invf[:, :], func=AF.Exp,
                                  scale=-_m.log(10000.0) / 32.0))

        @block.vector
        def _(v):
            s_a.wait(v)
            s_v.inc(v.tensor_scalar(TH[:, :], TH[:, :], invf[:, 0:1], None, ALU.mult))
            s_v.wait(v)
            s_v.inc(v.tensor_scalar(T1[:, :], TH[:, :], 1.0 / (2 * PI), None, ALU.mult))
            s_v.wait(v)
            s_v.inc(v.tensor_copy(TI[:, :], T1[:, :]))
            s_v.wait(v)
            s_v.inc(v.tensor_copy(T1[:, :], TI[:, :]))
            s_v.wait(v)
            s_v.inc(v.scalar_tensor_tensor(T2[:, :], T1[:, :], -2 * PI, TH[:, :],
                                           ALU.mult, ALU.add))

        @block.scalar
        def _(sc):
            s_v.wait(sc)
            s_a.inc(sc.activation(out=C2[:, :], in_=T2[:, :], func=AF.Sin))

        @block.vector
        def _(v):
            s_a.wait(v)
            s_v.inc(v.tensor_scalar(C2[:, :], C2[:, :], sgn[:, 0:1], None, ALU.mult))
            s_v.wait(v)
            s_v.inc(v.tensor_scalar(T1[:, :], TH[:, :], 1.0 / (2 * PI), 1.0 / 4.0,
                                    ALU.mult, ALU.add))
            s_v.wait(v)
            s_v.inc(v.tensor_copy(TI[:, :], T1[:, :]))
            s_v.wait(v)
            s_v.inc(v.tensor_copy(T1[:, :], TI[:, :]))
            s_v.wait(v)
            s_v.inc(v.scalar_tensor_tensor(T2[:, :], T1[:, :], -2 * PI, TH[:, :],
                                           ALU.mult, ALU.add))
            s_v.wait(v)
            s_v.inc(v.tensor_scalar(T2[:, :], T2[:, :], 1.0, PI / 2.0, ALU.mult, ALU.add))

        @block.scalar
        def _(sc):
            s_v.wait(sc)
            s_a.inc(sc.activation(out=C1[:, :], in_=T2[:, :], func=AF.Sin))

        ctx.barrier()
        tbl_scope.close()

        for b in range(NB):
            sc2b = contextlib.ExitStack()
            ps_sw = sc2b.enter_context(nc.psum_tensor(tag + "ps_sw%d" % b, [128, S], F32))
            ps_tr = sc2b.enter_context(nc.psum_tensor(tag + "ps_tr%d" % b, [128, 128], BF16))
            for c in range(FC):
                for which in ("k", "q"):
                    rows0 = 1024 if which == "k" else 0

                    @block.sync
                    def _(sy, c=c, rows0=rows0, b=b):
                        s_v.wait(sy)
                        s_a.wait(sy)
                        s_sp.wait(sy)
                        s_at.inc(sy.dma_start(
                            out=QIN[:, :],
                            in_=qkT_d[rows0 + c * 128:rows0 + (c + 1) * 128,
                                      b * S:(b + 1) * S]), 16)

                    @block.tensor
                    def _(t, c=c):
                        s_at.wait(t)
                        s_v.wait(t)
                        s_g.wait(t)
                        last = None
                        for nq in range(4):
                            last = t.matmul(ps_sw[:, nq * 512:(nq + 1) * 512], Pm[:, :],
                                            QIN[:, nq * 512:(nq + 1) * 512],
                                            start=True, stop=True)
                        s_pe.inc(last)

                    @block.vector
                    def _(v, which=which, c=c):
                        s_pe.wait(v)
                        s_a.wait(v)
                        sc_ = 0.125 if which == "q" else 1.0
                        s_v.inc(v.tensor_mul(T1[:, :], QIN[:, :], C1[:, :]))
                        s_v.wait(v)
                        s_v.inc(v.tensor_mul(T2[:, :], ps_sw[:, :], C2[:, :]))
                        s_v.wait(v)
                        s_v.inc(v.tensor_add(T1[:, :], T1[:, :], T2[:, :]))
                        s_v.wait(v)
                        s_v.inc(v.tensor_scalar(T2[:, :], T1[:, :], 0.0, sc_,
                                                ALU.min, ALU.mult))
                        s_v.wait(v)
                        s_v.inc(v.tensor_scalar(T1[:, :], T1[:, :], 0.0, sc_,
                                                ALU.max, ALU.mult))

                    @block.scalar
                    def _(sc):
                        s_v.wait(sc)
                        s_a.inc(sc.activation(out=T2[:, :], in_=T2[:, :], func=AF.Exp))

                    @block.vector
                    def _(v, which=which, c=c):
                        s_a.wait(v)
                        dst = KFC[:, :] if which == "k" else QF[:, c * S:(c + 1) * S]
                        s_v.inc(v.tensor_add(dst, T1[:, :], T2[:, :]))

                    if which == "k":
                        @block.tensor
                        def _(t, c=c):
                            s_v.wait(t)
                            for j in range(16):
                                if j > 0:
                                    t.wait_ge(s_v.h, s_v.n + j)
                                t.transpose(ps_tr[:, :], KFC[:, j * 128:(j + 1) * 128],
                                            IDN[:, :])
                                s_pe.inc(t.nop())

                        @block.vector
                        def _(v, c=c):
                            base = s_pe.n - 16
                            for j in range(16):
                                v.wait_ge(s_pe.h, base + j + 1)
                                s_v.inc(v.tensor_copy(
                                    KT[:, j * 1024 + c * 128:j * 1024 + (c + 1) * 128],
                                    ps_tr[:, :]))

            ctx.barrier()
            sc2b.close()

            with contextlib.ExitStack() as sc3:
                ps_kv = sc3.enter_context(nc.psum_tensor(tag + "ps_kv%d" % b, [64, 65], F32))
                ps_out = sc3.enter_context(nc.psum_tensor(tag + "ps_out%d" % b, [65, S], F32))
                ps_rb2 = sc3.enter_context(nc.psum_tensor(tag + "ps_rb2%d" % b, [64, 1024], F32))

                for h in range(H):
                    ib = h % 2

                    @block.sync
                    def _(sy, h=h, ib=ib, b=b):
                        s_pe.wait(sy)
                        s_g.wait(sy)
                        s_at.inc(sy.dma_start(
                            out=va[ib][:, :].rearrange("p (kc d) -> p kc d", kc=16)[:, :, 0:64],
                            in_=vtok_d[b * S:(b + 1) * S, h * 64:(h + 1) * 64]
                            .rearrange("(kc p) d -> p kc d", p=128)), 16)

                    @block.tensor
                    def _(t, h=h, ib=ib):
                        s_at.wait(t)
                        s_v.wait(t)
                        last = None
                        for j in range(16):
                            last = t.matmul(ps_kv[:, :],
                                            KT[:, j * 1024 + h * 64:j * 1024 + (h + 1) * 64],
                                            va[ib][:, j * 65:(j + 1) * 65],
                                            start=(j == 0), stop=(j == 15))
                        s_pe.inc(last)

                    @block.vector
                    def _(v):
                        s_pe.wait(v)
                        with nc.allow_low_precision(reason="bf16 matmul operand"):
                            s_v.inc(v.tensor_copy(kvs[:, :], ps_kv[:, :]))

                    if h % 2 == 1:
                        @block.gpsimd
                        def _(g):
                            s_v.wait(g)
                            s_at.inc(g.dma_start(out=kvs_hi[64:128, :], in_=kvs[:, :]), 16)

                    @block.tensor
                    def _(t, h=h):
                        s_v.wait(t)
                        s_at.wait(t)
                        ch, half64 = h // 2, (h % 2) * 64
                        lhs = kvs[:, :] if h % 2 == 0 else kvs_hi[64:128, :]
                        last = None
                        for nq in range(4):
                            last = t.matmul(ps_out[:, nq * 512:(nq + 1) * 512], lhs,
                                            QF[half64:half64 + 64,
                                               ch * S + nq * 512:ch * S + (nq + 1) * 512],
                                            start=True, stop=True)
                        s_pe.inc(last)

                    @block.vector
                    def _(v):
                        s_pe.wait(v)
                        s_v.inc(v.tensor_scalar(recf[0:1, :], ps_out[64:65, :], 1e-6,
                                                None, ALU.max))
                        s_v.wait(v)
                        with nc.allow_low_precision(reason="bf16 matmul operand"):
                            s_v.inc(v.reciprocal(rec_row[:, :], recf[0:1, :]))

                    for half in range(2):
                        @block.tensor
                        def _(t, half=half):
                            s_v.wait(t)
                            if half == 1:
                                s_a.wait(t)
                            t.matmul(ps_rb2[:, 0:512], ones_row[:, 0:64],
                                     rec_row[:, half * 1024:half * 1024 + 512],
                                     start=True, stop=True)
                            s_pe.inc(t.matmul(ps_rb2[:, 512:1024], ones_row[:, 0:64],
                                              rec_row[:, half * 1024 + 512:(half + 1) * 1024],
                                              start=True, stop=True))

                        @block.scalar
                        def _(sc, half=half):
                            s_pe.wait(sc)
                            s_v.wait(sc)
                            s_a.inc(sc.copy(out=rb_sb[:, :], in_=ps_rb2[:, :]))

                        @block.vector
                        def _(v, h=h, half=half):
                            s_a.wait(v)
                            ch, half64 = h // 2, (h % 2) * 64
                            s_v.inc(v.tensor_mul(
                                outn[half64:half64 + 64,
                                     ch * S + half * 1024:ch * S + (half + 1) * 1024],
                                ps_out[0:64, half * 1024:(half + 1) * 1024], rb_sb[:, :]))

            ctx.barrier()

            with contextlib.ExitStack() as sc4:
                XLD = ctx.sbuf(tag + "xld%d" % b, [128, S], F32, scope=sc4)
                OST = ctx.sbuf(tag + "ost%d" % b, [128, S], F32, scope=sc4)
                ps_o = [sc4.enter_context(nc.psum_tensor(tag + "ps_o%d_%d" % (b, i),
                                                          [128, 512], F32))
                        for i in range(2)]

                for ms in range(8):
                    @block.sync
                    def _(sy, ms=ms, b=b):
                        s_v.wait(sy)
                        s_g.inc(sy.dma_start(
                            out=XLD[:, :].rearrange("p (j n) -> p j n", n=256),
                            in_=xrow(ms * 128, b * S, S)), 16)

                    @block.tensor
                    def _(t, ms=ms):
                        s_v.wait(t)
                        if ms == 0:
                            s_ld.wait(t)
                        s_a.wait(t)
                        s_sp.wait(t)
                        for q4 in range(4):
                            ps = ps_o[q4 % 2]
                            if q4 >= 2:
                                t.wait_ge(s_v.h, s_v.n + q4 - 1)
                            last = None
                            for kc in range(FC):
                                last = t.matmul(
                                    ps[:, :],
                                    WOUT[:, kc * 1024 + ms * 128:kc * 1024 + (ms + 1) * 128],
                                    outn[:, kc * S + q4 * 512:kc * S + (q4 + 1) * 512],
                                    start=(kc == 0), stop=(kc == FC - 1))
                            s_pe.inc(last)

                    @block.vector
                    def _(v, ms=ms):
                        base = s_pe.n - 4
                        s_g.wait(v)
                        s_sp.wait(v)
                        for q4 in range(4):
                            ps = ps_o[q4 % 2]
                            v.wait_ge(s_pe.h, base + q4 + 1)
                            s_v.inc(v.tensor_add(OST[:, q4 * 512:(q4 + 1) * 512],
                                                 XLD[:, q4 * 512:(q4 + 1) * 512], ps[:, :]))

                    @block.scalar
                    def _(sc, ms=ms, b=b):
                        s_v.wait(sc)
                        s_sp.inc(sc.dma_start(
                            out=yout_d[ms * 128:(ms + 1) * 128, b * S:(b + 1) * S],
                            in_=OST[:, :]), 16)

            ctx.barrier()


def _run_l1_attn(x_t, w_qkv, w_out):
    nc = bass.Bass("TRN2", target_bir_lowering=False, debug=False)
    xin_h = nc.dram_tensor("xT", [D, NT], F32, kind="ExternalInput")
    wq_d = nc.dram_tensor("wq", [D, 3072], F32, kind="ExternalInput").ap()
    wo_d = nc.dram_tensor("wo", [D, D], F32, kind="ExternalInput").ap()
    yout = nc.dram_tensor("yT", [D, NT], F32, kind="ExternalOutput").ap()
    qkT_d = nc.dram_tensor("qkT", [2048, NT], BF16).ap()
    vtok_d = nc.dram_tensor("vtok", [NT, D], BF16).ap()

    with contextlib.ExitStack() as es:
        block = es.enter_context(nc.Block())
        ctx = Ctx(nc, block, es)
        xprov, xrow, _ = _prov_flat(xin_h)
        emit_l1_attn(ctx, xprov, xrow, yout, wq_d, wo_d, qkT_d, vtok_d)
        ctx.barrier(engines=("sync",))

    wqT = np.ascontiguousarray(np.asarray(w_qkv, np.float32).T)
    woT = np.ascontiguousarray(np.asarray(w_out, np.float32).T)
    in_maps = [{"xT": x_t[i], "wq": wqT, "wo": woT} for i in range(8)]
    res = run_bass_kernel_spmd(nc, in_maps, core_ids=list(range(8)))
    return np.stack([np.asarray(r["yT"]) for r in res.results])



def emit_l1_ffn(ctx, xprov, xrow, yout_d, wr_d, br_d, w1_d, b1_d, w2_d, b2_d, lnx_d,
                tag="m1"):
    nc, block = ctx.nc, ctx.block
    s_ld = ctx.sem(tag + "ld")
    s_w = ctx.sem(tag + "wl")
    s_sp = ctx.sem(tag + "sp")
    s_pe = ctx.sem(tag + "pe")
    s_v = ctx.sem(tag + "vv")
    s_a = ctx.sem(tag + "aa")
    s_g = ctx.sem(tag + "gg")

    ones2 = ctx.sbuf(tag + "ones2", [128, 2], BF16)
    onesf = ctx.sbuf(tag + "onesf", [128, 1], F32)
    ones_row = ctx.sbuf(tag + "onesr", [1, 128], BF16)

    @block.gpsimd
    def _(g):
        g.memset(ones2[:, 0:1], -1.0 / D)
        g.memset(ones2[:, 1:2], 1.0)
        g.memset(onesf[:, 0:1], -1.0 / D)
        s_g.inc(g.memset(ones_row[:, :], 1.0))

    with contextlib.ExitStack() as sca:
        XRC = ctx.sbuf(tag + "xrc", [128, FC * TCH], F32, scope=sca)
        XC = ctx.sbuf(tag + "xc", [128, FC * TCH], F32, scope=sca)
        LNB = ctx.sbuf(tag + "lnb", [128, FC * TCH], BF16, scope=sca)
        row1 = ctx.sbuf(tag + "row1", [1, TCH], BF16, scope=sca)
        rowr = ctx.sbuf(tag + "rowr", [1, TCH], BF16, scope=sca)
        rowv = ctx.sbuf(tag + "rowv", [1, TCH], F32, scope=sca)
        ps_sum = sca.enter_context(nc.psum_tensor(tag + "ps_sum", [1, TCH], F32))
        ps_mb = sca.enter_context(nc.psum_tensor(tag + "ps_mb", [128, TCH], F32))
        ps_rb = sca.enter_context(nc.psum_tensor(tag + "ps_rb", [128, TCH], F32))
        consts = (onesf, ones2, ones_row, ps_sum, ps_mb, ps_rb, row1, rowr, rowv)

        for tc in range(NTC):
            c0 = tc * TCH

            @block.sync
            def _(sy, c0=c0):
                s_a.wait(sy)
                s_v.wait(sy)
                s_pe.wait(sy)
                s_sp.wait(sy)
                s_g.inc(sy.dma_start(
                    out=XRC[:, :].rearrange("p (kc n) -> p kc n", kc=FC),
                    in_=xprov(c0, TCH)), 16)

            @block.tensor
            def _(t):
                s_g.wait(t)

            @block.vector
            def _(v):
                s_g.wait(v)

            @block.scalar
            def _(sc):
                s_g.wait(sc)

            emit_ln(ctx, XRC, LNB, XC, consts, 0, (s_pe, s_v, s_a), xstride=TCH)

            @block.sync
            def _(sy, c0=c0):
                s_v.wait(sy)
                s_sp.inc(sy.dma_start(
                    out=lnx_d.rearrange("(kc p) n -> p kc n", p=128)[:, :, c0:c0 + TCH],
                    in_=LNB[:, :].rearrange("p (kc n) -> p kc n", kc=FC)), 16)

        ctx.barrier()

    comb = ctx.sbuf(tag + "comb", [128, 32 * 16], F32)
    combT = ctx.sbuf(tag + "combT", [16, NT], BF16)
    SEL = ctx.sbuf(tag + "sel", [16, 16 * 128], BF16)

    with contextlib.ExitStack() as scb:
        WR = ctx.sbuf(tag + "wrr", [128, FC * 16], BF16, scope=scb)
        brb = ctx.sbuf(tag + "brb", [128, 16], F32, scope=scb)
        brrow = ctx.sbuf(tag + "brrow", [1, 16], BF16, scope=scb)
        LNL = ctx.sbuf(tag + "lnl", [128, FC * 128], BF16, scope=scb)
        lg = ctx.sbuf(tag + "lg", [128, 16], F32, scope=scb)
        pr2 = ctx.sbuf(tag + "pr2", [128, 16], F32, scope=scb)
        msk1 = ctx.sbuf(tag + "msk1", [128, 16], F32, scope=scb)
        msk2 = ctx.sbuf(tag + "msk2", [128, 16], F32, scope=scb)
        m1c = ctx.sbuf(tag + "m1c", [128, 1], F32, scope=scb)
        m2c = ctx.sbuf(tag + "m2c", [128, 1], F32, scope=scb)
        rcc = ctx.sbuf(tag + "rcc", [128, 1], F32, scope=scb)
        selio = ctx.sbuf(tag + "selio", [16, 16 * 128], F32, scope=scb)
        pi16 = ctx.sbuf(tag + "pi16", [16, 1], F32, scope=scb)
        ps_r = scb.enter_context(nc.psum_tensor(tag + "ps_r", [128, 16], F32))
        ps_t = scb.enter_context(nc.psum_tensor(tag + "ps_t", [16, 128], BF16))
        cmb_bf = ctx.sbuf(tag + "cmbbf", [128, 16], BF16, scope=scb)
        idn2 = ctx.sbuf(tag + "idn2", [128, 128], BF16, scope=scb)

        @block.gpsimd
        def _(g):
            s_ld.inc(g.dma_start(
                out=WR[:, :].rearrange("p (kc m) -> p kc m", kc=FC),
                in_=wr_d.rearrange("(kc p) m -> p kc m", p=128)), 16)
            s_ld.inc(g.dma_start(out=brrow[:, :], in_=br_d[0:1, :]), 16)
            g.iota(selio[:, :].rearrange("p (e n) -> p e n", e=16), [[1, 16], [0, 128]],
                   base=0, channel_multiplier=0, allow_small_or_imprecise_dtypes=True)
            g.memset(idn2[:, :], 0.0)
            g.affine_select(out=idn2[:, :], in_=idn2[:, :], compare_op=ALU.not_equal,
                            fill=1.0, base=0, pattern=[[-1, 128]], channel_multiplier=1)
            s_g.inc(g.iota(pi16[:, :], [[1, 1]], base=0, channel_multiplier=1,
                           allow_small_or_imprecise_dtypes=True))

        @block.vector
        def _(v):
            s_g.wait(v)
            s_v.inc(v.tensor_scalar(SEL[:, :], selio[:, :], pi16[:, 0:1], None,
                                    ALU.is_equal))

        @block.tensor
        def _(t):
            s_ld.wait(t)
            s_g.wait(t)
            s_pe.inc(t.matmul(ps_r[:, :], ones_row[:, :], brrow[:, :],
                              start=True, stop=True))

        @block.vector
        def _(v):
            s_pe.wait(v)
            s_v.inc(v.tensor_copy(brb[:, :], ps_r[:, :]))

        for ts in range(32):
            @block.sync
            def _(sy, ts=ts):
                s_pe.wait(sy)
                s_v.wait(sy)
                s_g.inc(sy.dma_start(
                    out=LNL[:, :].rearrange("p (kc n) -> p kc n", kc=FC),
                    in_=lnx_d.rearrange("(kc p) n -> p kc n", p=128)
                    [:, :, ts * 128:(ts + 1) * 128]), 16)

            @block.tensor
            def _(t, ts=ts):
                s_g.wait(t)
                s_v.wait(t)
                last = None
                for kc in range(FC):
                    last = t.matmul(ps_r[:, :], LNL[:, kc * 128:(kc + 1) * 128],
                                    WR[:, kc * 16:(kc + 1) * 16],
                                    start=(kc == 0), stop=(kc == FC - 1))
                s_pe.inc(last)

            @block.vector
            def _(v, ts=ts):
                s_pe.wait(v)
                s_a.wait(v)
                s_v.inc(v.tensor_add(lg[:, :], ps_r[:, :], brb[:, :]))
                s_v.wait(v)
                s_v.inc(v.reduce_max(m1c[:, :], lg[:, :], axis=AX.X))
                s_v.wait(v)
                s_v.inc(v.tensor_scalar(m2c[:, :], m1c[:, :], -1.0, None, ALU.mult))

            @block.scalar
            def _(sc):
                s_v.wait(sc)
                s_a.inc(sc.activation(out=lg[:, :], in_=lg[:, :], func=AF.Exp,
                                      bias=m2c[:, 0:1]))

            @block.vector
            def _(v, ts=ts):
                s_a.wait(v)
                s_v.inc(v.reduce_sum(m1c[:, :], lg[:, :], axis=AX.X))
                s_v.wait(v)
                s_v.inc(v.reciprocal(rcc[:, :], m1c[:, :]))
                s_v.wait(v)
                s_v.inc(v.tensor_scalar(lg[:, :], lg[:, :], rcc[:, 0:1], None, ALU.mult))
                s_v.wait(v)
                s_v.inc(v.reduce_max(m1c[:, :], lg[:, :], axis=AX.X))
                s_v.wait(v)
                s_v.inc(v.tensor_scalar(msk1[:, :], lg[:, :], m1c[:, 0:1], None,
                                        ALU.is_equal))
                s_v.wait(v)
                s_v.inc(v.scalar_tensor_tensor(pr2[:, :], msk1[:, :], -4.0, lg[:, :],
                                               ALU.mult, ALU.add))
                s_v.wait(v)
                s_v.inc(v.reduce_max(m2c[:, :], pr2[:, :], axis=AX.X))
                s_v.wait(v)
                s_v.inc(v.tensor_scalar(msk2[:, :], pr2[:, :], m2c[:, 0:1], None,
                                        ALU.is_equal))
                s_v.wait(v)
                s_v.inc(v.tensor_add(rcc[:, :], m1c[:, :], m2c[:, :]))
                s_v.wait(v)
                s_v.inc(v.tensor_scalar(rcc[:, :], rcc[:, :], 1e-9, None, ALU.max))
                s_v.wait(v)
                s_v.inc(v.reciprocal(rcc[:, :], rcc[:, :]))
                s_v.wait(v)
                s_v.inc(v.tensor_scalar(msk1[:, :], msk1[:, :], m1c[:, 0:1], None,
                                        ALU.mult))
                s_v.wait(v)
                s_v.inc(v.tensor_scalar(msk2[:, :], msk2[:, :], m2c[:, 0:1], None,
                                        ALU.mult))
                s_v.wait(v)
                s_v.inc(v.tensor_add(msk1[:, :], msk1[:, :], msk2[:, :]))
                s_v.wait(v)
                s_v.inc(v.tensor_scalar(comb[:, ts * 16:(ts + 1) * 16], msk1[:, :],
                                        rcc[:, 0:1], None, ALU.mult))
                s_v.wait(v)
                with nc.allow_low_precision(reason="bf16 matmul operand"):
                    s_v.inc(v.tensor_copy(cmb_bf[:, :], comb[:, ts * 16:(ts + 1) * 16]))

            @block.tensor
            def _(t, ts=ts):
                s_v.wait(t)
                s_g.wait(t)
                s_pe.inc(t.transpose(ps_t[:, 0:128], cmb_bf[:, :], idn2[:, :]))

            @block.vector
            def _(v, ts=ts):
                s_pe.wait(v)
                s_v.inc(v.tensor_copy(combT[:, ts * 128:(ts + 1) * 128], ps_t[0:16, 0:128]))

        ctx.barrier()

    with contextlib.ExitStack() as scc:
        W1B = [ctx.sbuf(tag + "w1b%d" % i, [128, FC * 2048], BF16, scope=scc)
               for i in range(2)]
        W2B = ctx.sbuf(tag + "w2b", [128, 16 * 1024], BF16, scope=scc)
        ACC = ctx.sbuf(tag + "acc", [128, FC * 1024], F32, scope=scc)
        LNX = ctx.sbuf(tag + "lnx3", [128, FC * 512], BF16, scope=scc)
        HSB = ctx.sbuf(tag + "hsb", [128, 16 * 512], BF16, scope=scc)
        GTMP = ctx.sbuf(tag + "gtmp", [128, 512], F32, scope=scc)
        B2S = ctx.sbuf(tag + "b2s", [16, 1024], BF16, scope=scc)
        b1c = ctx.sbuf(tag + "b1c", [128, 256], F32, scope=scc)
        XLD = ctx.sbuf(tag + "xld", [128, 512], F32, scope=scc)
        LNL2 = ctx.sbuf(tag + "lnl2", [128, 512], BF16, scope=scc)
        OST = ctx.sbuf(tag + "ost", [128, 512], F32, scope=scc)

        ps_h = [scc.enter_context(nc.psum_tensor(tag + "ps_h%d" % i, [128, 512], F32))
                for i in range(2)]
        ps_m = [scc.enter_context(nc.psum_tensor(tag + "ps_m%d" % i, [128, 512], F32))
                for i in range(2)]
        ps_cb = scc.enter_context(nc.psum_tensor(tag + "ps_cb", [128, 512], F32))

        @block.gpsimd
        def _(g):
            s_ld.inc(g.dma_start(out=B2S[:, :], in_=b2_d[:, :]), 16)
            s_ld.inc(g.dma_start(out=b1c[:, :], in_=b1_d[:, :]), 16)

        def load_w1(g, e, buf):
            s_w.inc(g.dma_start(
                out=W1B[buf][:, :].rearrange("p (kc m) -> p kc m", kc=FC),
                in_=w1_d[e].rearrange("(kc p) m -> p kc m", p=128)), 16)

        def load_w2(g, e):
            s_w.inc(g.dma_start(
                out=W2B[:, :].rearrange("p (hc m) -> p hc m", hc=16),
                in_=w2_d[e].rearrange("(hc p) m -> p hc m", p=128)), 16)

        NPASS = 4
        for half in range(NPASS):
            hc0 = half * 1024

            for tc in range(2):
                cbase = hc0 + tc * 512

                base_v0 = s_v.n

                @block.tensor
                def _(t, tc=tc, cbase=cbase, half=half, base_v0=base_v0):
                    s_v.wait(t)
                    s_ld.wait(t)
                    for os in range(8):
                        ps = ps_m[os % 2]
                        if os >= 2:
                            t.wait_ge(s_v.h, base_v0 + os - 1)
                        s_pe.inc(t.matmul(ps[:, :], B2S[:, os * 128:(os + 1) * 128],
                                          combT[:, cbase:cbase + 512],
                                          start=True, stop=True))

                @block.vector
                def _(v, tc=tc):
                    base = s_pe.n - 8
                    for os in range(8):
                        ps = ps_m[os % 2]
                        v.wait_ge(s_pe.h, base + os + 1)
                        s_v.inc(v.tensor_copy(ACC[:, os * 1024 + tc * 512:os * 1024 + (tc + 1) * 512],
                                              ps[:, :]))

            for e in range(16):
                @block.gpsimd
                def _(g, e=e, half=half):
                    if e == 0 and half == 0:
                        load_w1(g, 0, 0)
                        load_w2(g, 0)
                    s_pe.wait(g)
                    nxt = e + 1 + half * 16
                    if nxt < 16 * 4:
                        load_w1(g, (e + 1) % 16, (e + 1) % 2)

                @block.gpsimd
                def _(g, e=e, half=half):
                    if e > 0 or half > 0:
                        s_pe.wait(g)
                        load_w2(g, e)

                for tc in range(2):
                    cbase = hc0 + tc * 512

                    @block.sync
                    def _(sy, tc=tc, cbase=cbase):
                        s_pe.wait(sy)
                        s_g.inc(sy.dma_start(
                            out=LNX[:, :].rearrange("p (kc n) -> p kc n", kc=FC),
                            in_=lnx_d.rearrange("(kc p) n -> p kc n", p=128)
                            [:, :, cbase:cbase + 512]), 16)

                    @block.tensor
                    def _(t, tc=tc, cbase=cbase, e=e):
                        s_g.wait(t)
                        s_w.wait(t)
                        s_v.wait(t)
                        t.matmul(ps_cb[:, :], SEL[:, e * 128:(e + 1) * 128],
                                 combT[:, cbase:cbase + 512], start=True, stop=True)
                        s_pe.inc(t.nop())
                        base_a = s_a.n
                        for hs in range(16):
                            ps = ps_h[hs % 2]
                            if hs >= 2:
                                t.wait_ge(s_a.h, base_a + hs - 1)
                            last = None
                            for kc in range(FC):
                                last = t.matmul(
                                    ps[:, :],
                                    W1B[e % 2][:, kc * 2048 + hs * 128:kc * 2048 + (hs + 1) * 128],
                                    LNX[:, kc * 512:(kc + 1) * 512],
                                    start=(kc == 0), stop=(kc == FC - 1))
                            s_pe.inc(last)

                    @block.scalar
                    def _(sc, e=e):
                        base = s_pe.n - 16
                        base_v = s_v.n
                        for hs in range(16):
                            ps = ps_h[hs % 2]
                            sc.wait_ge(s_pe.h, base + hs + 1)
                            if hs > 0:
                                sc.wait_ge(s_v.h, base_v + hs)
                            s_a.inc(sc.activation(out=GTMP[:, :], in_=ps[:, :],
                                                  func=AF.Gelu,
                                                  bias=b1c[:, e * 16 + hs:e * 16 + hs + 1]))

                    @block.vector
                    def _(v, e=e):
                        base = s_a.n - 16
                        for hs in range(16):
                            v.wait_ge(s_a.h, base + hs + 1)
                            s_v.inc(v.tensor_mul(HSB[:, hs * 512:(hs + 1) * 512],
                                                 GTMP[:, :], ps_cb[:, :]))

                    @block.tensor
                    def _(t, tc=tc, e=e):
                        s_v.wait(t)
                        for os in range(8):
                            ps = ps_m[os % 2]
                            if os >= 2:
                                t.wait_ge(s_v.h, s_v.n + os - 1)
                            last = None
                            for hc in range(16):
                                last = t.matmul(
                                    ps[:, :],
                                    W2B[:, hc * 1024 + os * 128:hc * 1024 + (os + 1) * 128],
                                    HSB[:, hc * 512:(hc + 1) * 512],
                                    start=(hc == 0), stop=(hc == 15))
                            s_pe.inc(last)

                    @block.vector
                    def _(v, tc=tc):
                        base = s_pe.n - 8
                        for os in range(8):
                            ps = ps_m[os % 2]
                            v.wait_ge(s_pe.h, base + os + 1)
                            sl = ACC[:, os * 1024 + tc * 512:os * 1024 + (tc + 1) * 512]
                            s_v.inc(v.tensor_add(sl, sl, ps[:, :]))

            for os in range(8):
                for tc in range(2):
                    cbase = hc0 + tc * 512

                    @block.sync
                    def _(sy, os=os, tc=tc, cbase=cbase):
                        s_v.wait(sy)
                        s_sp.wait(sy)
                        s_g.inc(sy.dma_start(out=XLD[:, :].rearrange("p (j n) -> p j n", n=256),
                                             in_=xrow(os * 128, cbase, 512)), 16)
                        s_g.inc(sy.dma_start(out=LNL2[:, :],
                                             in_=lnx_d[os * 128:(os + 1) * 128,
                                                       cbase:cbase + 512]), 16)

                    @block.vector
                    def _(v, os=os, tc=tc):
                        s_g.wait(v)
                        s_v.inc(v.tensor_add(OST[:, :], XLD[:, :], LNL2[:, :]))
                        s_v.wait(v)
                        s_v.inc(v.tensor_add(OST[:, :], OST[:, :],
                                             ACC[:, os * 1024 + tc * 512:os * 1024 + (tc + 1) * 512]))

                    @block.scalar
                    def _(sc, os=os, cbase=cbase):
                        s_v.wait(sc)
                        s_sp.inc(sc.dma_start(
                            out=yout_d[os * 128:(os + 1) * 128, cbase:cbase + 512],
                            in_=OST[:, :]), 16)

            ctx.barrier()


def _run_l1_ffn(x_t, Wr, br, W1, b1, W2, b2):
    nc = bass.Bass("TRN2", target_bir_lowering=False, debug=False)
    xin_h = nc.dram_tensor("xT", [D, NT], F32, kind="ExternalInput")
    wr_d = nc.dram_tensor("wr", [D, 16], F32, kind="ExternalInput").ap()
    br_d = nc.dram_tensor("br", [1, 16], F32, kind="ExternalInput").ap()
    w1_d = nc.dram_tensor("w1", [16, D, 2048], F32, kind="ExternalInput").ap()
    b1_d = nc.dram_tensor("b1", [128, 256], F32, kind="ExternalInput").ap()
    w2_d = nc.dram_tensor("w2", [16, 2048, D], F32, kind="ExternalInput").ap()
    b2_d = nc.dram_tensor("b2", [16, D], F32, kind="ExternalInput").ap()
    yout = nc.dram_tensor("yT", [D, NT], F32, kind="ExternalOutput").ap()
    lnx_d = nc.dram_tensor("lnx", [D, NT], BF16).ap()

    with contextlib.ExitStack() as es:
        block = es.enter_context(nc.Block())
        ctx = Ctx(nc, block, es)
        xprov, xrow, _ = _prov_flat(xin_h)
        emit_l1_ffn(ctx, xprov, xrow, yout, wr_d, br_d, w1_d, b1_d, w2_d, b2_d, lnx_d)
        ctx.barrier(engines=("sync",))

    b1c = np.ascontiguousarray(np.asarray(b1, np.float32).reshape(16, 16, 128)
                               .transpose(2, 0, 1).reshape(128, 256))
    in_maps = [{"xT": x_t[i], "wr": np.asarray(Wr, np.float32),
                "br": np.asarray(br, np.float32).reshape(1, 16), "w1": W1, "b1": b1c,
                "w2": W2, "b2": b2} for i in range(8)]
    res = run_bass_kernel_spmd(nc, in_maps, core_ids=list(range(8)))
    return np.stack([np.asarray(r["yT"]) for r in res.results])



def _run_l0_attn(x_t, wq, bq, wo, bo):
    nc = bass.Bass("TRN2", target_bir_lowering=False, debug=False)
    xin = nc.dram_tensor("xT", [D, NT], F32, kind="ExternalInput").ap()
    wq_d = nc.dram_tensor("wq", [D, 3072], F32, kind="ExternalInput").ap()
    bq_d = nc.dram_tensor("bq", [128, 24], F32, kind="ExternalInput").ap()
    wo_d = nc.dram_tensor("wo", [D, D], F32, kind="ExternalInput").ap()
    bo_d = nc.dram_tensor("bo", [128, 8], F32, kind="ExternalInput").ap()
    yout = nc.dram_tensor("yT", [D, NT], F32, kind="ExternalOutput").ap()
    qkT_d = nc.dram_tensor("qkT", [2048, NT], BF16).ap()
    vtok_d = nc.dram_tensor("vtok", [NT, D], BF16).ap()

    with contextlib.ExitStack() as es:
        block = es.enter_context(nc.Block())
        ctx = Ctx(nc, block, es)
        XR = ctx.sbuf("XR", [128, FC * NT], F32)
        s_x = ctx.sem("zx")
        s_out = ctx.sem("zo")

        @block.sync
        def _(sy):
            s_x.inc(sy.dma_start(out=XR[:, :].rearrange("p (kc n) -> p kc n", kc=FC),
                                 in_=xin.rearrange("(kc p) n -> p kc n", p=128)), 16)

        for en in ("tensor", "vector", "scalar", "gpsimd"):
            def mk(eng):
                s_x.wait(eng)
            getattr(block, en)(mk)

        emit_l0_attn(ctx, XR, wq_d, bq_d, wo_d, bo_d, qkT_d, vtok_d)

        ctx.barrier(engines=("sync",))

        @block.sync
        def _(sy):
            s_out.inc(sy.dma_start(out=yout.rearrange("(kc p) n -> p kc n", p=128),
                                   in_=XR[:, :].rearrange("p (kc n) -> p kc n", kc=FC)), 16)
            s_out.wait(sy)

    bq2 = np.ascontiguousarray(bq.reshape(24, 128).T)
    bo2 = np.ascontiguousarray(bo.reshape(8, 128).T)
    in_maps = [{"xT": x_t[i], "wq": wq, "bq": bq2, "wo": wo, "bo": bo2}
               for i in range(8)]
    res = run_bass_kernel_spmd(nc, in_maps, core_ids=list(range(8)))
    return np.stack([np.asarray(r["yT"]) for r in res.results])


def _to_fm(x):
    return np.ascontiguousarray(
        x.reshape(8, NB, S, D).transpose(0, 3, 1, 2).reshape(8, D, NT))


def _from_fm(y_t):
    return np.ascontiguousarray(
        y_t.reshape(8, D, NB, S).transpose(0, 2, 3, 1).reshape(16, S, D))


def _run_l1_both(x_t, ins):
    nc = bass.Bass("TRN2", target_bir_lowering=False, debug=False)
    F = lambda nm, sh: nc.dram_tensor(nm, sh, F32, kind="ExternalInput").ap()
    xin_h = nc.dram_tensor("xT", [D, NT], F32, kind="ExternalInput")
    b_wq = F("b_wq", [D, 3072])
    b_wo = F("b_wo", [D, D])
    m_wr = F("m_wr", [D, 16])
    m_br = F("m_br", [1, 16])
    m_w1 = F("m_w1", [16, D, 2048])
    m_b1 = F("m_b1", [128, 256])
    m_w2 = F("m_w2", [16, 2048, D])
    m_b2 = F("m_b2", [16, D])
    yout = nc.dram_tensor("yT", [D, NT], F32, kind="ExternalOutput").ap()

    qkT1 = nc.dram_tensor("qkT1", [2048, NT], BF16).ap()
    vtok1 = nc.dram_tensor("vtok1", [NT, D], BF16).ap()
    lnxm = nc.dram_tensor("lnxm", [D, NT], BF16).ap()
    y3 = nc.dram_tensor("y3", [D, NT], F32)

    with contextlib.ExitStack() as es:
        block = es.enter_context(nc.Block())
        ctx = Ctx(nc, block, es)
        xprov, xrow, _ = _prov_flat(xin_h)
        emit_l1_attn(ctx, xprov, xrow, y3.ap(), b_wq, b_wo, qkT1, vtok1)
        ctx.barrier()
        xp_f, xr_f, _ = _prov_flat(y3)
        emit_l1_ffn(ctx, xp_f, xr_f, yout, m_wr, m_br, m_w1, m_b1, m_w2, m_b2, lnxm)
        ctx.barrier(engines=("sync",))

    tcol = lambda a: np.ascontiguousarray(np.asarray(a, np.float32).T)
    f32 = lambda k: np.asarray(ins[k], np.float32)
    b1c = np.ascontiguousarray(f32("L1_ffn_b1").reshape(16, 16, 128)
                               .transpose(2, 0, 1).reshape(128, 256))
    base = {"b_wq": tcol(ins["L1_attn_w_qkv"]), "b_wo": tcol(ins["L1_attn_w_out"]),
            "m_wr": f32("L1_ffn_Wr"), "m_br": f32("L1_ffn_br").reshape(1, 16),
            "m_w1": f32("L1_ffn_W1"), "m_b1": b1c, "m_w2": f32("L1_ffn_W2"),
            "m_b2": f32("L1_ffn_b2")}
    in_maps = [dict(base, xT=x_t[i]) for i in range(8)]
    res = run_bass_kernel_spmd(nc, in_maps, core_ids=list(range(8)))
    return np.stack([np.asarray(r["yT"]) for r in res.results])


def _bs_to_ss(y):
    return np.ascontiguousarray(
        y.reshape(8, D, 2, 8, 256).transpose(3, 1, 0, 2, 4).reshape(8, D, NT))


def _ss_to_bs(y):
    return np.ascontiguousarray(
        y.reshape(8, D, 8, 2, 256).transpose(2, 1, 3, 0, 4).reshape(8, D, NT))


def kernel(**inputs):
    f32 = lambda k: np.asarray(inputs[k], np.float32)
    x_t = _to_fm(f32("L0_attn_x"))
    y1 = _run_l0_attn(x_t, f32("L0_attn_Wqkv"), f32("L0_attn_bqkv"),
                      f32("L0_attn_Wout"), f32("L0_attn_bout"))
    y2 = _run_l0_ffn(_bs_to_ss(y1),
                     f32("L0_ffn_wf"), f32("L0_ffn_uf"), f32("L0_ffn_bf"),
                     f32("L0_ffn_wi"), f32("L0_ffn_ui"), f32("L0_ffn_bi"),
                     f32("L0_ffn_wc"), f32("L0_ffn_uc"), f32("L0_ffn_bc"),
                     f32("L0_ffn_wo"), f32("L0_ffn_uo"), f32("L0_ffn_bo"),
                     f32("L0_ffn_in_c"))
    y4 = _run_l1_both(_ss_to_bs(y2), inputs)
    return _from_fm(y4)


def kernel_unfused(**inputs):
    f32 = lambda k: np.asarray(inputs[k], np.float32)
    x_t = _to_fm(f32("L0_attn_x"))
    y1 = _run_l0_attn(x_t, f32("L0_attn_Wqkv"), f32("L0_attn_bqkv"),
                      f32("L0_attn_Wout"), f32("L0_attn_bout"))
    y2 = _run_l0_ffn(_bs_to_ss(y1),
                     f32("L0_ffn_wf"), f32("L0_ffn_uf"), f32("L0_ffn_bf"),
                     f32("L0_ffn_wi"), f32("L0_ffn_ui"), f32("L0_ffn_bi"),
                     f32("L0_ffn_wc"), f32("L0_ffn_uc"), f32("L0_ffn_bc"),
                     f32("L0_ffn_wo"), f32("L0_ffn_uo"), f32("L0_ffn_bo"),
                     f32("L0_ffn_in_c"))
    y3 = _run_l1_attn(_ss_to_bs(y2), f32("L1_attn_w_qkv"), f32("L1_attn_w_out"))
    y4 = _run_l1_ffn(y3, f32("L1_ffn_Wr"), f32("L1_ffn_br"), f32("L1_ffn_W1"),
                     f32("L1_ffn_b1"), f32("L1_ffn_W2"), f32("L1_ffn_b2"))
    return _from_fm(y4)


# version 06
# speedup vs baseline: 1.1009x; 1.1009x over previous
import contextlib
import numpy as np
import concourse.bass as bass
import concourse.mybir as mybir
from concourse.bass_utils import run_bass_kernel_spmd

F32 = mybir.dt.float32
BF16 = mybir.dt.bfloat16
I32 = mybir.dt.int32
AF = mybir.ActivationFunctionType
ALU = mybir.AluOpType
AX = mybir.AxisListType

D = 1024
NT = 4096
NB = 2
S = 2048
H = 16
HD = 64
FC = 8
TCH = 256
NTC = NT // TCH
EPS = 1e-5


class Sem:
    def __init__(self, h):
        self.h = h
        self.n = 0

    def inc(self, instr, k=1):
        instr.then_inc(self.h, k)
        self.n += k
        return instr

    def wait(self, eng):
        eng.wait_ge(self.h, self.n)


class Ctx:
    def __init__(self, nc, block, exit_stack):
        self.nc = nc
        self.block = block
        self.es = exit_stack
        self._sems = {}

    def sem(self, name):
        if name not in self._sems:
            self._sems[name] = Sem(self.es.enter_context(self.nc.semaphore(name)))
        return self._sems[name]

    def sbuf(self, name, shape, dtype, scope=None):
        return (scope or self.es).enter_context(self.nc.sbuf_tensor(name, shape, dtype))

    def barrier(self, engines=("tensor", "vector", "scalar", "gpsimd", "sync")):
        sems = list(self._sems.values())
        for en in engines:
            def mk(eng, sems=sems):
                for s in sems:
                    s.wait(eng)
            getattr(self.block, en)(mk)



def emit_ln(ctx, XR, LNB, XC, consts, tc, sems, xstride=None):
    nc, block = ctx.nc, ctx.block
    s_pe, s_v, s_a = sems
    onesf, ones2, ones_row, ps_sum, ps_mb, ps_rb, row1, rowr, rowv = consts
    NTs = NT if xstride is None else xstride
    c0 = tc * TCH

    @block.tensor
    def _(t):
        s_v.wait(t)
        s_a.wait(t)
        last = None
        for kc in range(FC):
            last = t.matmul(ps_sum[:, :], onesf[:, 0:1], XR[:, kc * NTs + c0:kc * NTs + c0 + TCH],
                            start=(kc == 0), stop=(kc == FC - 1))
        s_pe.inc(last)

    @block.scalar
    def _(sc):
        s_pe.wait(sc)
        s_a.inc(sc.copy(out=row1[:, :], in_=ps_sum[0:1, :]))

    @block.tensor
    def _(t):
        s_a.wait(t)
        s_pe.inc(t.matmul(ps_mb[:, :], ones_row[:, :], row1[:, :], start=True, stop=True))

    @block.vector
    def _(v):
        s_pe.wait(v)
        last = None
        for kc in range(FC):
            last = v.tensor_add(XC[:, kc * TCH:(kc + 1) * TCH],
                                XR[:, kc * NTs + c0:kc * NTs + c0 + TCH], ps_mb[:, :])
        s_v.inc(last)

    @block.scalar
    def _(sc):
        s_v.wait(sc)
        last = None
        for kc in range(FC):
            last = sc.activation(out=LNB[:, kc * TCH:(kc + 1) * TCH],
                                 in_=XC[:, kc * TCH:(kc + 1) * TCH], func=AF.Square)
        s_a.inc(last)

    @block.tensor
    def _(t):
        s_a.wait(t)
        last = None
        for kc in range(FC):
            last = t.matmul(ps_sum[:, :], ones2[:, 1:2], LNB[:, kc * TCH:(kc + 1) * TCH],
                            start=(kc == 0), stop=(kc == FC - 1))
        s_pe.inc(last)

    @block.vector
    def _(v):
        s_pe.wait(v)
        s_v.inc(v.tensor_scalar(rowv[:, :], ps_sum[0:1, :], 1.0 / D, EPS, ALU.mult, ALU.add))

    @block.scalar
    def _(sc):
        s_v.wait(sc)
        s_a.inc(sc.activation(out=rowv[:, :], in_=rowv[:, :], func=AF.Ln))
        s_a.wait(sc)
        s_a.inc(sc.activation(out=rowr[:, :], in_=rowv[:, :], func=AF.Exp, scale=-0.5))

    @block.tensor
    def _(t):
        s_a.wait(t)
        s_pe.inc(t.matmul(ps_rb[:, :], ones_row[:, :], rowr[:, :], start=True, stop=True))

    @block.vector
    def _(v):
        s_pe.wait(v)
        last = None
        for kc in range(FC):
            last = v.tensor_mul(LNB[:, kc * TCH:(kc + 1) * TCH],
                                XC[:, kc * TCH:(kc + 1) * TCH], ps_rb[:, :])
        s_v.inc(last)


def emit_l0_attn(ctx, XR, wq_d, bq_d, wo_d, bo_d, qkT_d, vtok_d, tag="a0",
                 alloc_scope=None):
    nc, block = ctx.nc, ctx.block
    osc = alloc_scope
    s_ld = ctx.sem(tag + "ld")
    s_sp = ctx.sem(tag + "sp")
    s_at = ctx.sem(tag + "at")
    s_pe = ctx.sem(tag + "pe")
    s_v = ctx.sem(tag + "vv")
    s_a = ctx.sem(tag + "aa")
    s_g = ctx.sem(tag + "gg")

    ones2 = ctx.sbuf(tag + "ones2", [128, 2], BF16, scope=osc)
    onesf = ctx.sbuf(tag + "onesf", [128, 1], F32, scope=osc)
    ones_row = ctx.sbuf(tag + "onesr", [1, 128], BF16, scope=osc)
    bq_sb = ctx.sbuf(tag + "bq", [128, 16], F32, scope=osc)
    bv_sb = ctx.sbuf(tag + "bv", [128, 8], BF16, scope=osc)
    bo_sb = ctx.sbuf(tag + "bo", [128, 8], F32, scope=osc)

    with contextlib.ExitStack() as sc1:
        WQ = ctx.sbuf(tag + "wq", [128, FC * 3072], BF16, scope=sc1)
        XC = ctx.sbuf(tag + "xcen", [128, FC * TCH], F32, scope=sc1)
        LNB = ctx.sbuf(tag + "lnb", [128, FC * TCH], BF16, scope=sc1)
        STG = ctx.sbuf(tag + "stg", [128, 8 * TCH], BF16, scope=sc1)
        VSTG = ctx.sbuf(tag + "vstg", [128, 2 * 1024], BF16, scope=sc1)
        row1 = ctx.sbuf(tag + "row1", [1, TCH], BF16, scope=sc1)
        rowr = ctx.sbuf(tag + "rowr", [1, TCH], BF16, scope=sc1)
        rowv = ctx.sbuf(tag + "rowv", [1, TCH], F32, scope=sc1)

        ps_sum = sc1.enter_context(nc.psum_tensor(tag + "ps_sum", [1, TCH], F32))
        ps_mb = sc1.enter_context(nc.psum_tensor(tag + "ps_mb", [128, TCH], F32))
        ps_rb = sc1.enter_context(nc.psum_tensor(tag + "ps_rb", [128, TCH], F32))
        ps_q0 = sc1.enter_context(nc.psum_tensor(tag + "ps_q0", [128, TCH], F32))
        ps_q1 = sc1.enter_context(nc.psum_tensor(tag + "ps_q1", [128, TCH], F32))
        ps_v0 = sc1.enter_context(nc.psum_tensor(tag + "ps_v0", [128, 512], F32))
        ps_v1 = sc1.enter_context(nc.psum_tensor(tag + "ps_v1", [128, 512], F32))

        consts = (onesf, ones2, ones_row, ps_sum, ps_mb, ps_rb, row1, rowr, rowv)

        @block.gpsimd
        def _(g):
            s_ld.inc(g.dma_start(
                out=WQ[:, :].rearrange("p (kc m) -> p kc m", kc=FC),
                in_=wq_d.rearrange("(kc p) m -> p kc m", p=128)), 16)
            s_ld.inc(g.dma_start(out=bq_sb[:, :], in_=bq_d[:, 0:16]), 16)
            s_ld.inc(g.dma_start(out=bv_sb[:, :], in_=bq_d[:, 16:24]), 16)
            s_ld.inc(g.dma_start(out=bo_sb[:, :], in_=bo_d[:, :]), 16)
            g.memset(ones2[:, 0:1], -1.0 / D)
            g.memset(ones2[:, 1:2], 1.0)
            g.memset(onesf[:, 0:1], -1.0 / D)
            s_g.inc(g.memset(ones_row[:, :], 1.0))

        @block.tensor
        def _(t):
            s_g.wait(t)
            s_ld.wait(t)

        for tc in range(NTC):
            c0 = tc * TCH
            emit_ln(ctx, XR, LNB, XC, consts, tc, (s_pe, s_v, s_a))

            for grp in range(2):
                base_v = s_v.n
                base_pe = s_pe.n

                @block.tensor
                def _(t, grp=grp, base_v=base_v):
                    s_v.wait(t)
                    s_sp.wait(t)
                    for j in range(8):
                        ms = grp * 8 + j
                        ps = ps_q0 if j % 2 == 0 else ps_q1
                        if j >= 2:
                            t.wait_ge(s_v.h, base_v + j - 1)
                        last = None
                        for kc in range(FC):
                            last = t.matmul(ps[:, :],
                                            WQ[:, kc * 3072 + ms * 128:kc * 3072 + (ms + 1) * 128],
                                            LNB[:, kc * TCH:(kc + 1) * TCH],
                                            start=(kc == 0), stop=(kc == FC - 1))
                        s_pe.inc(last)

                @block.vector
                def _(v, grp=grp, base_pe=base_pe):
                    for j in range(8):
                        ms = grp * 8 + j
                        ps = ps_q0 if j % 2 == 0 else ps_q1
                        v.wait_ge(s_pe.h, base_pe + j + 1)
                        s_v.inc(v.tensor_scalar(STG[:, j * TCH:(j + 1) * TCH], ps[:, :],
                                                bq_sb[:, ms:ms + 1], None, ALU.add))

                @block.sync
                def _(sy, c0=c0, grp=grp):
                    s_v.wait(sy)
                    s_sp.inc(sy.dma_start(
                        out=qkT_d.rearrange("(ms p) n -> p ms n", p=128)[:, grp * 8:(grp + 1) * 8,
                                                                        c0:c0 + TCH],
                        in_=STG[:, :].rearrange("p (ms n) -> p ms n", ms=8)), 16)

            base_pe = s_pe.n
            base_v = s_v.n

            @block.tensor
            def _(t, base_v=base_v):
                s_sp.wait(t)
                for i in range(4):
                    ts, half = divmod(i, 2)
                    ps = ps_v0 if i % 2 == 0 else ps_v1
                    if i >= 2:
                        t.wait_ge(s_v.h, base_v + i - 1)
                    last = None
                    for kc in range(FC):
                        last = t.matmul(
                            ps[:, :],
                            LNB[:, kc * TCH + ts * 128:kc * TCH + (ts + 1) * 128],
                            WQ[:, kc * 3072 + 2048 + half * 512:kc * 3072 + 2048 + (half + 1) * 512],
                            start=(kc == 0), stop=(kc == FC - 1))
                    s_pe.inc(last)

            @block.vector
            def _(v, base_pe=base_pe):
                for i in range(4):
                    ts, half = divmod(i, 2)
                    ps = ps_v0 if i % 2 == 0 else ps_v1
                    v.wait_ge(s_pe.h, base_pe + i + 1)
                    s_v.inc(v.tensor_copy(
                        VSTG[:, ts * 1024 + half * 512:ts * 1024 + (half + 1) * 512], ps[:, :]))

            @block.scalar
            def _(sc, c0=c0):
                s_v.wait(sc)
                s_sp.inc(sc.dma_start(
                    out=vtok_d[c0:c0 + TCH, :].rearrange("(ts p) n -> p ts n", p=128),
                    in_=VSTG[:, :].rearrange("p (ts n) -> p ts n", ts=2)), 16)

        ctx.barrier()

    QG = 1024

    with contextlib.ExitStack() as sc2:
        WO = ctx.sbuf(tag + "wo", [128, FC * 1024], BF16, scope=sc2)
        qa = [ctx.sbuf(tag + "qa%d" % i, [65, QG], BF16, scope=sc2) for i in range(2)]
        ka = [ctx.sbuf(tag + "ka%d" % i, [65, S], BF16, scope=sc2) for i in range(2)]
        va = [ctx.sbuf(tag + "va%d" % i, [128, 16 * 65], BF16, scope=sc2) for i in range(2)]
        ex = [ctx.sbuf(tag + "ex%d" % i, [128, QG], BF16, scope=sc2) for i in range(2)]
        rb_sb = ctx.sbuf(tag + "rb", [64, QG], BF16, scope=sc2)
        rec_row = ctx.sbuf(tag + "rec", [1, QG], BF16, scope=sc2)
        outn = ctx.sbuf(tag + "outn", [128, FC * QG], BF16, scope=sc2)
        nsq = ctx.sbuf(tag + "nsq", [64, S], BF16, scope=sc2)
        knrow = ctx.sbuf(tag + "knrow", [1, S], BF16, scope=sc2)
        knm2 = ctx.sbuf(tag + "knm2", [1, 1], F32, scope=sc2)
        b64 = ctx.sbuf(tag + "b64", [1, QG], BF16, scope=sc2)

        ps_s = [sc2.enter_context(nc.psum_tensor(tag + "ps_s%d" % i, [128, QG], F32))
                for i in range(2)]
        ps_av = sc2.enter_context(nc.psum_tensor(tag + "ps_av", [65, QG], F32))
        ps_rbb = sc2.enter_context(nc.psum_tensor(tag + "ps_rbb", [64, QG], F32))
        ps_bo = ps_s[0]

        @block.gpsimd
        def _(g):
            s_ld.inc(g.dma_start(
                out=WO[:, :].rearrange("p (kc m) -> p kc m", kc=FC),
                in_=wo_d.rearrange("(kc p) m -> p kc m", p=128)), 16)
            for i in range(2):
                g.memset(va[i][:, :], 1.0)
                s_g.inc(g.memset(ka[i][64:65, :], 1.0))

        for ms in range(8):
            @block.tensor
            def _(t, ms=ms):
                if ms == 0:
                    s_ld.wait(t)
                else:
                    s_v.wait(t)
                last = None
                for kc in range(FC):
                    last = t.matmul(ps_bo[:, 0:1],
                                    WO[:, kc * 1024 + ms * 128:kc * 1024 + (ms + 1) * 128],
                                    bv_sb[:, kc:kc + 1],
                                    start=(kc == 0), stop=(kc == FC - 1))
                s_pe.inc(last)

            @block.vector
            def _(v, ms=ms):
                s_pe.wait(v)
                s_v.inc(v.tensor_add(bo_sb[:, ms:ms + 1], bo_sb[:, ms:ms + 1], ps_bo[:, 0:1]))

        for b in range(NB):
            for qg in range(S // QG):
                for h in range(H):
                    ib = h % 2

                    @block.sync
                    def _(sy, h=h, ib=ib, b=b, qg=qg):
                        s_pe.wait(sy)
                        s_sp.wait(sy)
                        s_g.wait(sy)
                        s_at.inc(sy.dma_start(
                            out=qa[ib][0:64, :],
                            in_=qkT_d[h * 64:(h + 1) * 64,
                                      b * S + qg * QG:b * S + (qg + 1) * QG]), 16)
                        s_at.inc(sy.dma_start(
                            out=ka[ib][0:64, :],
                            in_=qkT_d[1024 + h * 64:1024 + (h + 1) * 64, b * S:(b + 1) * S]), 16)
                        s_at.inc(sy.dma_start(
                            out=va[ib][:, :].rearrange("p (kc d) -> p kc d", kc=16)[:, :, 0:64],
                            in_=vtok_d[b * S:(b + 1) * S, h * 64:(h + 1) * 64]
                            .rearrange("(kc p) d -> p kc d", p=128)), 16)

                    @block.scalar
                    def _(sc, ib=ib):
                        s_at.wait(sc)
                        s_v.wait(sc)
                        s_a.inc(sc.activation(out=nsq[:, :], in_=ka[ib][0:64, :],
                                              func=AF.Square))

                    for g2 in range(2):
                        @block.tensor
                        def _(t, g2=g2):
                            s_a.wait(t)
                            s_v.wait(t)
                            last = None
                            for half in range(2):
                                last = t.matmul(
                                    ps_rbb[0:1, half * 512:(half + 1) * 512],
                                    ones2[0:64, 1:2],
                                    nsq[:, g2 * 1024 + half * 512:g2 * 1024 + (half + 1) * 512],
                                    start=True, stop=True)
                            s_pe.inc(last)

                        @block.vector
                        def _(v, g2=g2):
                            s_pe.wait(v)
                            s_v.inc(v.tensor_copy(knrow[:, g2 * 1024:(g2 + 1) * 1024],
                                                  ps_rbb[0:1, 0:1024]))

                    @block.vector
                    def _(v):
                        s_v.wait(v)
                        s_v.inc(v.reduce_max(knm2[:, 0:1], knrow[:, :], axis=AX.X))

                    @block.scalar
                    def _(sc, ib=ib):
                        s_v.wait(sc)
                        s_a.inc(sc.activation(out=nsq[0:64, 0:QG], in_=qa[ib][0:64, :],
                                              func=AF.Square))

                    @block.tensor
                    def _(t):
                        s_a.wait(t)
                        s_v.wait(t)
                        last = None
                        for half in range(2):
                            last = t.matmul(ps_rbb[0:1, half * 512:(half + 1) * 512],
                                            ones2[0:64, 1:2],
                                            nsq[:, half * 512:(half + 1) * 512],
                                            start=True, stop=True)
                        s_pe.inc(last)

                    @block.vector
                    def _(v):
                        s_pe.wait(v)
                        s_v.inc(v.tensor_scalar(b64[0:1, :], ps_rbb[0:1, :], knm2[0:1, 0:1],
                                                -0.5, ALU.add, ALU.mult))

                    @block.gpsimd
                    def _(g, ib=ib):
                        s_v.wait(g)
                        s_at.inc(g.dma_start(out=qa[ib][64:65, :], in_=b64[0:1, :]), 16)

                    base_a = s_a.n
                    base_pe = s_pe.n

                    def score_sec(kc, ib=ib, base_a=base_a):
                        @block.tensor
                        def _(t, kc=kc, ib=ib):
                            if kc == 0:
                                s_v.wait(t)
                                s_at.wait(t)
                                s_a.wait(t)
                            elif kc >= 2:
                                t.wait_ge(s_a.h, base_a + kc - 1)
                            last = None
                            for half in range(2):
                                last = t.matmul(
                                    ps_s[kc % 2][:, half * 512:(half + 1) * 512],
                                    ka[ib][:, kc * 128:(kc + 1) * 128],
                                    qa[ib][:, half * 512:(half + 1) * 512],
                                    start=True, stop=True)
                            s_pe.inc(last)

                    def exp_sec(kc, base_pe=base_pe):
                        @block.scalar
                        def _(sc, kc=kc):
                            w = 1 if kc == 0 else (2 if kc == 1 else 2 * kc)
                            sc.wait_ge(s_pe.h, base_pe + w)
                            s_a.inc(sc.activation(out=ex[kc % 2][:, :],
                                                  in_=ps_s[kc % 2][:, :],
                                                  func=AF.Exp, scale=0.125))

                    def av_sec(kc, ib=ib, base_a=base_a):
                        @block.tensor
                        def _(t, kc=kc, ib=ib):
                            t.wait_ge(s_a.h, base_a + kc + 1)
                            last = None
                            for half in range(2):
                                last = t.matmul(
                                    ps_av[:, half * 512:(half + 1) * 512],
                                    va[ib][:, kc * 65:(kc + 1) * 65],
                                    ex[kc % 2][:, half * 512:(half + 1) * 512],
                                    start=(kc == 0), stop=(kc == 15))
                            s_pe.inc(last)

                    score_sec(0)
                    exp_sec(0)
                    score_sec(1)
                    exp_sec(1)
                    for kc in range(2, 16):
                        av_sec(kc - 2)
                        score_sec(kc)
                        exp_sec(kc)
                    av_sec(14)
                    av_sec(15)

                    @block.vector
                    def _(v):
                        s_pe.wait(v)
                        with nc.allow_low_precision(reason="bf16 matmul operand"):
                            s_v.inc(v.reciprocal(rec_row[:, :], ps_av[64:65, :]))

                    @block.tensor
                    def _(t):
                        s_v.wait(t)
                        t.matmul(ps_rbb[:, 0:512], ones_row[:, 0:64], rec_row[:, 0:512],
                                 start=True, stop=True)
                        s_pe.inc(t.matmul(ps_rbb[:, 512:1024], ones_row[:, 0:64],
                                          rec_row[:, 512:1024], start=True, stop=True))

                    @block.scalar
                    def _(sc):
                        s_pe.wait(sc)
                        s_a.inc(sc.copy(out=rb_sb[:, :], in_=ps_rbb[:, :]))

                    @block.vector
                    def _(v, h=h):
                        s_a.wait(v)
                        ch = h // 2
                        rowbase = (h % 2) * 64
                        s_v.inc(v.tensor_mul(outn[rowbase:rowbase + 64, ch * QG:(ch + 1) * QG],
                                             ps_av[0:64, :], rb_sb[:, :]))

                base_v = s_v.n

                @block.tensor
                def _(t, base_v=base_v):
                    s_v.wait(t)
                    s_a.wait(t)
                    for ms in range(8):
                        ps = ps_s[ms % 2]
                        if ms >= 2:
                            t.wait_ge(s_v.h, base_v + ms - 1)
                        last = None
                        for half in range(2):
                            for kc in range(FC):
                                last = t.matmul(
                                    ps[:, half * 512:(half + 1) * 512],
                                    WO[:, kc * 1024 + ms * 128:kc * 1024 + (ms + 1) * 128],
                                    outn[:, kc * QG + half * 512:kc * QG + (half + 1) * 512],
                                    start=(kc == 0), stop=(kc == FC - 1))
                        s_pe.inc(last)

                base_pe = s_pe.n - 8

                @block.vector
                def _(v, b=b, qg=qg, base_pe=base_pe):
                    for ms in range(8):
                        ps = ps_s[ms % 2]
                        v.wait_ge(s_pe.h, base_pe + ms + 1)
                        last = None
                        for half in range(2):
                            col = b * S + qg * QG + half * 512
                            sl = XR[:, ms * NT + col:ms * NT + col + 512]
                            last = v.scalar_tensor_tensor(
                                sl, ps[:, half * 512:(half + 1) * 512],
                                bo_sb[:, ms:ms + 1], sl, ALU.add, ALU.add)
                        s_v.inc(last)

                ctx.barrier()




def _prov_flat(h):
    def xprov(c0, w):
        return bass.AP(h, c0, [[4096, 128], [128 * 4096, 8], [1, w]])

    def xrow(f0, c0, w):
        return bass.AP(h, f0 * 4096 + c0, [[4096, 128], [256, w // 256], [1, 256]])

    def yprov(c0, w):
        return bass.AP(h, c0, [[4096, 128], [128 * 4096, 8], [1, w]])
    return xprov, xrow, yprov


def _prov_a2_ss(h):
    def xprov(c0, w):
        assert w == 256 and c0 % 256 == 0
        t = c0 // 256
        off = (t // 2) * 1024 * 512 + (t % 2) * 256
        return bass.AP(h, off, [[512, 128], [128 * 512, 8], [1, 256]])
    return xprov, None, xprov


def _prov_a2_bs(h):
    def xprov(c0, w):
        assert w == 256 and c0 % 256 == 0
        tb = c0 // 256
        b, j = tb // 8, tb % 8
        off = j * 1024 * 512 + b * 256
        return bass.AP(h, off, [[512, 128], [128 * 512, 8], [1, 256]])

    def xrow(f0, c0, w):
        tb = c0 // 256
        b, j0 = tb // 8, tb % 8
        k = w // 256
        assert j0 + k <= 8
        off = (j0 * 1024 + f0) * 512 + b * 256
        return bass.AP(h, off, [[512, 128], [1024 * 512, k], [1, 256]])
    return xprov, xrow, xprov



TS_FF = 256


def emit_l0_ffn(ctx, xprov, yprov, wT_d, uT_d, bcol_d, inc_d, lnx_d, gd_d, tag="f0"):
    nc, block = ctx.nc, ctx.block
    s_ld = ctx.sem(tag + "ld")
    s_sp = ctx.sem(tag + "sp")
    s_pe = ctx.sem(tag + "pe")
    s_v = ctx.sem(tag + "vv")
    s_a = ctx.sem(tag + "aa")
    s_g = ctx.sem(tag + "gg")

    ones2 = ctx.sbuf(tag + "ones2", [128, 2], BF16)
    onesf = ctx.sbuf(tag + "onesf", [128, 1], F32)
    ones_row = ctx.sbuf(tag + "onesr", [1, 128], BF16)
    bcol = ctx.sbuf(tag + "bcol", [128, 32], F32)
    incc = ctx.sbuf(tag + "incc", [128, 8], F32)

    @block.gpsimd
    def _(g):
        s_ld.inc(g.dma_start(out=bcol[:, :], in_=bcol_d[:, :]), 16)
        s_ld.inc(g.dma_start(out=incc[:, :], in_=inc_d[:, :]), 16)
        g.memset(ones2[:, 0:1], -1.0 / D)
        g.memset(ones2[:, 1:2], 1.0)
        g.memset(onesf[:, 0:1], -1.0 / D)
        s_g.inc(g.memset(ones_row[:, :], 1.0))

    with contextlib.ExitStack() as sca:
        XRC = ctx.sbuf(tag + "xrc", [128, FC * TCH], F32, scope=sca)
        XC = ctx.sbuf(tag + "xc", [128, FC * TCH], F32, scope=sca)
        LNB = ctx.sbuf(tag + "lnb", [128, FC * TCH], BF16, scope=sca)
        row1 = ctx.sbuf(tag + "row1", [1, TCH], BF16, scope=sca)
        rowr = ctx.sbuf(tag + "rowr", [1, TCH], BF16, scope=sca)
        rowv = ctx.sbuf(tag + "rowv", [1, TCH], F32, scope=sca)
        ps_sum = sca.enter_context(nc.psum_tensor(tag + "ps_sum", [1, TCH], F32))
        ps_mb = sca.enter_context(nc.psum_tensor(tag + "ps_mb", [128, TCH], F32))
        ps_rb = sca.enter_context(nc.psum_tensor(tag + "ps_rb", [128, TCH], F32))
        consts = (onesf, ones2, ones_row, ps_sum, ps_mb, ps_rb, row1, rowr, rowv)

        for tc in range(NTC):
            c0 = tc * TCH

            @block.sync
            def _(sy, c0=c0):
                s_a.wait(sy)
                s_v.wait(sy)
                s_pe.wait(sy)
                s_sp.wait(sy)
                s_g.inc(sy.dma_start(
                    out=XRC[:, :].rearrange("p (kc n) -> p kc n", kc=FC),
                    in_=xprov(c0, TCH)), 16)

            @block.tensor
            def _(t):
                s_g.wait(t)
                s_ld.wait(t)

            @block.vector
            def _(v):
                s_g.wait(v)

            @block.scalar
            def _(sc):
                s_g.wait(sc)

            emit_ln(ctx, XRC, LNB, XC, consts, 0, (s_pe, s_v, s_a), xstride=TCH)

            @block.sync
            def _(sy, c0=c0):
                s_v.wait(sy)
                s_sp.inc(sy.dma_start(
                    out=lnx_d.rearrange("(kc p) n -> p kc n", p=128)[:, :, c0:c0 + TCH],
                    in_=LNB[:, :].rearrange("p (kc n) -> p kc n", kc=FC)), 16)

        ctx.barrier()

    with contextlib.ExitStack() as scb:
        WG = ctx.sbuf(tag + "wg", [128, 4 * FC * 1024], BF16, scope=scb)
        LNL = ctx.sbuf(tag + "lnl", [128, FC * TCH], BF16, scope=scb)
        STG = ctx.sbuf(tag + "stgb", [128, FC * TCH], F32, scope=scb)
        ps_p = [scb.enter_context(nc.psum_tensor(tag + "ps_p%d" % i, [128, TCH], F32))
                for i in range(2)]

        @block.gpsimd
        def _(g):
            for w in range(4):
                s_ld.inc(g.dma_start(
                    out=WG[:, w * 8192:(w + 1) * 8192].rearrange("p (kc m) -> p kc m", kc=FC),
                    in_=wT_d[w].rearrange("(kc p) m -> p kc m", p=128)), 16)

        for tc in range(NTC):
            c0 = tc * TCH

            @block.sync
            def _(sy, c0=c0):
                s_pe.wait(sy)
                s_ld.wait(sy)
                s_g.inc(sy.dma_start(
                    out=LNL[:, :].rearrange("p (kc n) -> p kc n", kc=FC),
                    in_=lnx_d.rearrange("(kc p) n -> p kc n", p=128)[:, :, c0:c0 + TCH]), 16)

            for w in range(4):
                base_v = s_v.n
                base_pe = s_pe.n

                @block.tensor
                def _(t, w=w, base_v=base_v):
                    s_g.wait(t)
                    s_sp.wait(t)
                    for ms in range(8):
                        ps = ps_p[ms % 2]
                        if ms >= 2:
                            t.wait_ge(s_v.h, base_v + ms - 1)
                        last = None
                        for kc in range(FC):
                            last = t.matmul(
                                ps[:, :],
                                WG[:, w * 8192 + kc * 1024 + ms * 128:w * 8192 + kc * 1024 + (ms + 1) * 128],
                                LNL[:, kc * TCH:(kc + 1) * TCH],
                                start=(kc == 0), stop=(kc == FC - 1))
                        s_pe.inc(last)

                @block.vector
                def _(v, w=w, base_pe=base_pe):
                    for ms in range(8):
                        ps = ps_p[ms % 2]
                        v.wait_ge(s_pe.h, base_pe + ms + 1)
                        s_v.inc(v.tensor_scalar(STG[:, ms * TCH:(ms + 1) * TCH], ps[:, :],
                                                bcol[:, w * 8 + ms:w * 8 + ms + 1], None,
                                                ALU.add))

                @block.sync
                def _(sy, w=w, c0=c0):
                    s_v.wait(sy)
                    s_sp.inc(sy.dma_start(
                        out=gd_d[w].rearrange("(kc p) n -> p kc n", p=128)[:, :, c0:c0 + TCH],
                        in_=STG[:, :].rearrange("p (kc n) -> p kc n", kc=FC)), 16)

        ctx.barrier()

    CC = ctx.sbuf(tag + "cc", [128, FC * TS_FF], F32)
    CB = ctx.sbuf(tag + "cb", [128, FC * TS_FF], BF16)
    with contextlib.ExitStack() as scc:
        U3 = ctx.sbuf(tag + "u3", [128, 3 * FC * 1024], BF16, scope=scc)
        GA = [ctx.sbuf(tag + "ga%d" % w, [128, FC * TS_FF], F32, scope=scc) for w in range(3)]
        ps_p = [scc.enter_context(nc.psum_tensor(tag + "ps_c%d" % i, [128, TS_FF], F32))
                for i in range(2)]

        @block.gpsimd
        def _(g):
            for w in range(3):
                s_ld.inc(g.dma_start(
                    out=U3[:, w * 8192:(w + 1) * 8192].rearrange("p (kc m) -> p kc m", kc=FC),
                    in_=uT_d[w].rearrange("(kc p) m -> p kc m", p=128)), 16)

        @block.vector
        def _(v):
            s_ld.wait(v)
            last = None
            for kc in range(FC):
                last = v.tensor_copy(CC[:, kc * TS_FF:(kc + 1) * TS_FF],
                                     bass.AP(incc, kc, [[8, 128], [0, TS_FF]]))
            s_v.inc(last)
            s_v.wait(v)
            last = None
            for kc in range(FC):
                last = v.tensor_copy(CB[:, kc * TS_FF:(kc + 1) * TS_FF],
                                     CC[:, kc * TS_FF:(kc + 1) * TS_FF])
            s_v.inc(last)

        for t_ in range(16):
            ct = t_ * TS_FF

            @block.sync
            def _(sy, ct=ct):
                s_a.wait(sy)
                s_v.wait(sy)
                for w in range(3):
                    s_g.inc(sy.dma_start(
                        out=GA[w][:, :].rearrange("p (kc n) -> p kc n", kc=FC),
                        in_=gd_d[w].rearrange("(kc p) n -> p kc n", p=128)[:, :, ct:ct + TS_FF]),
                        16)

            for w in range(3):
                base_v = s_v.n
                base_pe = s_pe.n

                @block.tensor
                def _(t, w=w, base_v=base_v):
                    s_v.wait(t)
                    s_g.wait(t)
                    for ms in range(8):
                        ps = ps_p[ms % 2]
                        if ms >= 2:
                            t.wait_ge(s_v.h, base_v + ms - 1)
                        last = None
                        for kc in range(FC):
                            last = t.matmul(
                                ps[:, :],
                                U3[:, w * 8192 + kc * 1024 + ms * 128:w * 8192 + kc * 1024 + (ms + 1) * 128],
                                CB[:, kc * TS_FF:(kc + 1) * TS_FF],
                                start=(kc == 0), stop=(kc == FC - 1))
                        s_pe.inc(last)

                @block.vector
                def _(v, w=w, base_pe=base_pe):
                    for ms in range(8):
                        ps = ps_p[ms % 2]
                        v.wait_ge(s_pe.h, base_pe + ms + 1)
                        sl = GA[w][:, ms * TS_FF:(ms + 1) * TS_FF]
                        s_v.inc(v.scalar_tensor_tensor(sl, ps[:, :], 1.0, sl,
                                                       ALU.mult, ALU.add))

            @block.scalar
            def _(sc):
                s_v.wait(sc)
                sc.activation(out=GA[0][:, :], in_=GA[0][:, :], func=AF.Sigmoid)
                sc.activation(out=GA[1][:, :], in_=GA[1][:, :], func=AF.Sigmoid)
                s_a.inc(sc.activation(out=GA[2][:, :], in_=GA[2][:, :], func=AF.Tanh))

            @block.vector
            def _(v):
                s_a.wait(v)
                last = None
                for kc in range(FC):
                    sl = slice(kc * TS_FF, (kc + 1) * TS_FF)
                    last = v.tensor_mul(CC[:, sl], CC[:, sl], GA[0][:, sl])
                s_v.inc(last)
                s_v.wait(v)
                for kc in range(FC):
                    sl = slice(kc * TS_FF, (kc + 1) * TS_FF)
                    last = v.tensor_mul(GA[1][:, sl], GA[1][:, sl], GA[2][:, sl])
                s_v.inc(last)
                s_v.wait(v)
                for kc in range(FC):
                    sl = slice(kc * TS_FF, (kc + 1) * TS_FF)
                    last = v.tensor_add(CC[:, sl], CC[:, sl], GA[1][:, sl])
                s_v.inc(last)
                s_v.wait(v)
                for kc in range(FC):
                    sl = slice(kc * TS_FF, (kc + 1) * TS_FF)
                    last = v.tensor_copy(CB[:, sl], CC[:, sl])
                s_v.inc(last)

        ctx.barrier()

    with contextlib.ExitStack() as scd:
        UOW = ctx.sbuf(tag + "uow", [128, FC * 1024], BF16, scope=scd)
        UOT = ctx.sbuf(tag + "uot", [128, FC * TS_FF], F32, scope=scd)
        HOC = ctx.sbuf(tag + "hoc", [128, FC * TS_FF], F32, scope=scd)
        XRC2 = ctx.sbuf(tag + "xrc2", [128, FC * TS_FF], F32, scope=scd)
        ps_p = [scd.enter_context(nc.psum_tensor(tag + "ps_d%d" % i, [128, TS_FF], F32))
                for i in range(2)]

        @block.gpsimd
        def _(g):
            s_ld.inc(g.dma_start(
                out=UOW[:, :].rearrange("p (kc m) -> p kc m", kc=FC),
                in_=uT_d[3].rearrange("(kc p) m -> p kc m", p=128)), 16)

        base_v = s_v.n
        base_pe = s_pe.n

        @block.tensor
        def _(t, base_v=base_v):
            s_v.wait(t)
            s_ld.wait(t)
            for ms in range(8):
                ps = ps_p[ms % 2]
                if ms >= 2:
                    t.wait_ge(s_v.h, base_v + ms - 1)
                last = None
                for kc in range(FC):
                    last = t.matmul(ps[:, :],
                                    UOW[:, kc * 1024 + ms * 128:kc * 1024 + (ms + 1) * 128],
                                    CB[:, kc * TS_FF:(kc + 1) * TS_FF],
                                    start=(kc == 0), stop=(kc == FC - 1))
                s_pe.inc(last)

        @block.vector
        def _(v, base_pe=base_pe):
            for ms in range(8):
                ps = ps_p[ms % 2]
                v.wait_ge(s_pe.h, base_pe + ms + 1)
                s_v.inc(v.tensor_copy(UOT[:, ms * TS_FF:(ms + 1) * TS_FF], ps[:, :]))

        for t_ in range(16):
            ct = t_ * TS_FF

            @block.sync
            def _(sy, ct=ct):
                s_v.wait(sy)
                s_sp.wait(sy)
                s_g.inc(sy.dma_start(
                    out=HOC[:, :].rearrange("p (kc n) -> p kc n", kc=FC),
                    in_=gd_d[3].rearrange("(kc p) n -> p kc n", p=128)[:, :, ct:ct + TS_FF]), 16)
                s_g.inc(sy.dma_start(
                    out=XRC2[:, :].rearrange("p (kc n) -> p kc n", kc=FC),
                    in_=xprov(ct, TS_FF)), 16)

            @block.vector
            def _(v):
                s_g.wait(v)
                last = None
                for kc in range(FC):
                    sl = slice(kc * TS_FF, (kc + 1) * TS_FF)
                    last = v.tensor_add(HOC[:, sl], HOC[:, sl], UOT[:, sl])
                s_v.inc(last)

            @block.scalar
            def _(sc):
                s_v.wait(sc)
                s_a.inc(sc.activation(out=HOC[:, :], in_=HOC[:, :], func=AF.Tanh))

            @block.vector
            def _(v):
                s_a.wait(v)
                last = None
                for kc in range(FC):
                    sl = slice(kc * TS_FF, (kc + 1) * TS_FF)
                    last = v.tensor_add(XRC2[:, sl], XRC2[:, sl], HOC[:, sl])
                s_v.inc(last)

            @block.sync
            def _(sy, ct=ct):
                s_v.wait(sy)
                s_sp.inc(sy.dma_start(
                    out=yprov(ct, TS_FF),
                    in_=XRC2[:, :].rearrange("p (kc n) -> p kc n", kc=FC)), 16)

        ctx.barrier()


def _run_l0_ffn(x_t, wf, uf, bf, wi, ui, bi, wc, uc, bc, wo, uo, bo, in_c):
    nc = bass.Bass("TRN2", target_bir_lowering=False, debug=False)
    xin_h = nc.dram_tensor("xT", [D, NT], F32, kind="ExternalInput")
    xin = xin_h.ap()
    wT, uT = [], []
    for nm in ("wf", "wi", "wc", "wo"):
        wT.append(nc.dram_tensor(nm, [D, D], F32, kind="ExternalInput").ap())
    for nm in ("uf", "ui", "uc", "uo"):
        uT.append(nc.dram_tensor(nm, [D, D], F32, kind="ExternalInput").ap())
    bcol_d = nc.dram_tensor("bcol", [128, 32], F32, kind="ExternalInput").ap()
    inc_d = nc.dram_tensor("incc", [128, 8], F32, kind="ExternalInput").ap()
    yout_h = nc.dram_tensor("yT", [D, NT], F32, kind="ExternalOutput")
    lnx_d = nc.dram_tensor("lnx", [D, NT], BF16).ap()
    gd = [nc.dram_tensor("gd%d" % i, [D, NT], F32).ap() for i in range(4)]

    with contextlib.ExitStack() as es:
        block = es.enter_context(nc.Block())
        ctx = Ctx(nc, block, es)
        xprov = _prov_flat(xin_h)[0]
        yprov = _prov_flat(yout_h)[2]
        emit_l0_ffn(ctx, xprov, yprov, wT, uT, bcol_d, inc_d, lnx_d, gd)

        ctx.barrier(engines=("sync",))

    def tcol(a):
        return np.ascontiguousarray(np.asarray(a, np.float32).T)

    bcol = np.ascontiguousarray(
        np.stack([np.asarray(b, np.float32).reshape(8, 128) for b in (bf, bi, bc, bo)])
        .reshape(32, 128).T)
    incc = np.ascontiguousarray(np.asarray(in_c, np.float32).reshape(8, 128).T)
    in_maps = [{"xT": x_t[i], "wf": tcol(wf), "wi": tcol(wi), "wc": tcol(wc),
                "wo": tcol(wo), "uf": tcol(uf), "ui": tcol(ui), "uc": tcol(uc),
                "uo": tcol(uo), "bcol": bcol, "incc": incc} for i in range(8)]
    res = run_bass_kernel_spmd(nc, in_maps, core_ids=list(range(8)))
    return np.stack([np.asarray(r["yT"]) for r in res.results])



def emit_l1_attn(ctx, xprov, xrow, yout_d, wqT_d, woT_d, qkT_d, vtok_d, tag="b1"):
    nc, block = ctx.nc, ctx.block
    s_ld = ctx.sem(tag + "ld")
    s_sp = ctx.sem(tag + "sp")
    s_at = ctx.sem(tag + "at")
    s_pe = ctx.sem(tag + "pe")
    s_v = ctx.sem(tag + "vv")
    s_a = ctx.sem(tag + "aa")
    s_g = ctx.sem(tag + "gg")
    PI = 3.14159265358979323846

    ones2 = ctx.sbuf(tag + "ones2", [128, 2], BF16)
    onesf = ctx.sbuf(tag + "onesf", [128, 1], F32)
    ones_row = ctx.sbuf(tag + "onesr", [1, 128], BF16)

    @block.gpsimd
    def _(g):
        g.memset(ones2[:, 0:1], -1.0 / D)
        g.memset(ones2[:, 1:2], 1.0)
        g.memset(onesf[:, 0:1], -1.0 / D)
        s_g.inc(g.memset(ones_row[:, :], 1.0))

    with contextlib.ExitStack() as sc1:
        WQ = ctx.sbuf(tag + "wq", [128, FC * 3072], BF16, scope=sc1)
        XRC = ctx.sbuf(tag + "xrc", [128, FC * TCH], F32, scope=sc1)
        XC = ctx.sbuf(tag + "xcen", [128, FC * TCH], F32, scope=sc1)
        LNB = ctx.sbuf(tag + "lnb", [128, FC * TCH], BF16, scope=sc1)
        STG = ctx.sbuf(tag + "stg", [128, 8 * TCH], BF16, scope=sc1)
        VSTG = ctx.sbuf(tag + "vstg", [128, 2 * 1024], BF16, scope=sc1)
        row1 = ctx.sbuf(tag + "row1", [1, TCH], BF16, scope=sc1)
        rowr = ctx.sbuf(tag + "rowr", [1, TCH], BF16, scope=sc1)
        rowv = ctx.sbuf(tag + "rowv", [1, TCH], F32, scope=sc1)

        ps_sum = sc1.enter_context(nc.psum_tensor(tag + "ps_sum", [1, TCH], F32))
        ps_mb = sc1.enter_context(nc.psum_tensor(tag + "ps_mb", [128, TCH], F32))
        ps_rb = sc1.enter_context(nc.psum_tensor(tag + "ps_rb", [128, TCH], F32))
        ps_q0 = sc1.enter_context(nc.psum_tensor(tag + "ps_q0", [128, TCH], F32))
        ps_q1 = sc1.enter_context(nc.psum_tensor(tag + "ps_q1", [128, TCH], F32))
        ps_v0 = sc1.enter_context(nc.psum_tensor(tag + "ps_v0", [128, 512], F32))
        ps_v1 = sc1.enter_context(nc.psum_tensor(tag + "ps_v1", [128, 512], F32))
        consts = (onesf, ones2, ones_row, ps_sum, ps_mb, ps_rb, row1, rowr, rowv)

        @block.gpsimd
        def _(g):
            s_ld.inc(g.dma_start(
                out=WQ[:, :].rearrange("p (kc m) -> p kc m", kc=FC),
                in_=wqT_d.rearrange("(kc p) m -> p kc m", p=128)), 16)

        for tc in range(NTC):
            c0 = tc * TCH

            @block.sync
            def _(sy, c0=c0):
                s_a.wait(sy)
                s_v.wait(sy)
                s_pe.wait(sy)
                s_sp.wait(sy)
                s_g.inc(sy.dma_start(
                    out=XRC[:, :].rearrange("p (kc n) -> p kc n", kc=FC),
                    in_=xprov(c0, TCH)), 16)

            @block.tensor
            def _(t):
                s_g.wait(t)
                s_ld.wait(t)

            @block.vector
            def _(v):
                s_g.wait(v)

            @block.scalar
            def _(sc):
                s_g.wait(sc)

            emit_ln(ctx, XRC, LNB, XC, consts, 0, (s_pe, s_v, s_a), xstride=TCH)

            for grp in range(2):
                base_v = s_v.n
                base_pe = s_pe.n

                @block.tensor
                def _(t, grp=grp, base_v=base_v):
                    s_v.wait(t)
                    s_sp.wait(t)
                    for j in range(8):
                        ms = grp * 8 + j
                        ps = ps_q0 if j % 2 == 0 else ps_q1
                        if j >= 2:
                            t.wait_ge(s_v.h, base_v + j - 1)
                        last = None
                        for kc in range(FC):
                            last = t.matmul(ps[:, :],
                                            WQ[:, kc * 3072 + ms * 128:kc * 3072 + (ms + 1) * 128],
                                            LNB[:, kc * TCH:(kc + 1) * TCH],
                                            start=(kc == 0), stop=(kc == FC - 1))
                        s_pe.inc(last)

                @block.vector
                def _(v, grp=grp, base_pe=base_pe):
                    for j in range(8):
                        ps = ps_q0 if j % 2 == 0 else ps_q1
                        v.wait_ge(s_pe.h, base_pe + j + 1)
                        s_v.inc(v.tensor_copy(STG[:, j * TCH:(j + 1) * TCH], ps[:, :]))

                @block.sync
                def _(sy, c0=c0, grp=grp):
                    s_v.wait(sy)
                    s_sp.inc(sy.dma_start(
                        out=qkT_d.rearrange("(ms p) n -> p ms n", p=128)[:, grp * 8:(grp + 1) * 8,
                                                                        c0:c0 + TCH],
                        in_=STG[:, :].rearrange("p (ms n) -> p ms n", ms=8)), 16)

            base_pe = s_pe.n
            base_v = s_v.n

            @block.tensor
            def _(t, base_v=base_v):
                s_sp.wait(t)
                for i in range(4):
                    ts, half = divmod(i, 2)
                    ps = ps_v0 if i % 2 == 0 else ps_v1
                    if i >= 2:
                        t.wait_ge(s_v.h, base_v + i - 1)
                    last = None
                    for kc in range(FC):
                        last = t.matmul(
                            ps[:, :],
                            LNB[:, kc * TCH + ts * 128:kc * TCH + (ts + 1) * 128],
                            WQ[:, kc * 3072 + 2048 + half * 512:kc * 3072 + 2048 + (half + 1) * 512],
                            start=(kc == 0), stop=(kc == FC - 1))
                    s_pe.inc(last)

            @block.vector
            def _(v, base_pe=base_pe):
                for i in range(4):
                    ts, half = divmod(i, 2)
                    ps = ps_v0 if i % 2 == 0 else ps_v1
                    v.wait_ge(s_pe.h, base_pe + i + 1)
                    s_v.inc(v.tensor_copy(
                        VSTG[:, ts * 1024 + half * 512:ts * 1024 + (half + 1) * 512], ps[:, :]))

            @block.scalar
            def _(sc, c0=c0):
                s_v.wait(sc)
                s_sp.inc(sc.dma_start(
                    out=vtok_d[c0:c0 + TCH, :].rearrange("(ts p) n -> p ts n", p=128),
                    in_=VSTG[:, :].rearrange("p (ts n) -> p ts n", ts=2)), 16)

        ctx.barrier()

    with contextlib.ExitStack() as sc2:
        C1 = ctx.sbuf(tag + "c1", [128, S], F32, scope=sc2)
        C2 = ctx.sbuf(tag + "c2", [128, S], F32, scope=sc2)
        T1 = ctx.sbuf(tag + "t1", [128, S], F32, scope=sc2)
        T2 = ctx.sbuf(tag + "t2", [128, S], F32, scope=sc2)
        QIN = ctx.sbuf(tag + "qin", [128, S], BF16, scope=sc2)
        KFC = ctx.sbuf(tag + "kfc", [128, S], BF16, scope=sc2)
        Pm = ctx.sbuf(tag + "pm", [128, 128], BF16, scope=sc2)
        IDN = ctx.sbuf(tag + "idn", [128, 128], BF16, scope=sc2)
        pi32 = ctx.sbuf(tag + "pi32", [128, 1], I32, scope=sc2)
        sgn = ctx.sbuf(tag + "sgn", [128, 1], F32, scope=sc2)
        tsw = ctx.sbuf(tag + "tsw", [128, 1], F32, scope=sc2)
        invf = ctx.sbuf(tag + "invf", [128, 1], F32, scope=sc2)
        QF = ctx.sbuf(tag + "qf", [128, FC * S], BF16, scope=sc2)
        KT = ctx.sbuf(tag + "kt", [128, 16 * 1024], BF16, scope=sc2)
        va = [ctx.sbuf(tag + "va%d" % i, [128, 16 * 65], BF16, scope=sc2) for i in range(2)]
        kvs = ctx.sbuf(tag + "kvs", [64, 65], BF16, scope=sc2)
        kvs_hi = ctx.sbuf(tag + "kvsh", [128, 65], BF16, scope=sc2)
        outn = ctx.sbuf(tag + "outn", [128, FC * S], BF16, scope=sc2)
        rec_row = ctx.sbuf(tag + "rec", [1, S], BF16, scope=sc2)
        rb_sb = ctx.sbuf(tag + "rb", [64, 1024], BF16, scope=sc2)
        WOUT = ctx.sbuf(tag + "wout", [128, FC * 1024], BF16, scope=sc2)
        tbl_scope = contextlib.ExitStack()
        TH = ctx.sbuf(tag + "th", [128, S], F32, scope=tbl_scope)
        TI = ctx.sbuf(tag + "ti", [128, S], I32, scope=tbl_scope)
        i5 = ctx.sbuf(tag + "i5", [128, 1], I32, scope=tbl_scope)
        ib1 = ctx.sbuf(tag + "ib1", [128, 1], I32, scope=tbl_scope)
        iw = ctx.sbuf(tag + "iw", [128, 1], I32, scope=tbl_scope)
        recf = T1

        @block.gpsimd
        def _(g):
            s_ld.inc(g.dma_start(
                out=WOUT[:, :].rearrange("p (kc m) -> p kc m", kc=FC),
                in_=woT_d.rearrange("(kc p) m -> p kc m", p=128)), 16)
            g.iota(pi32[:, :], [[1, 1]], base=0, channel_multiplier=1)
            g.iota(TH[:, :], [[1, S]], base=0, channel_multiplier=0,
                   allow_small_or_imprecise_dtypes=True)
            g.memset(IDN[:, :], 0.0)
            g.affine_select(out=IDN[:, :], in_=IDN[:, :], compare_op=ALU.not_equal,
                            fill=1.0, base=0, pattern=[[-1, 128]], channel_multiplier=1)
            for i in range(2):
                g.memset(va[i][:, :], 1.0)
            s_g.inc(g.nop())

        @block.vector
        def _(v):
            s_g.wait(v)
            s_v.inc(v.tensor_scalar(i5[:, :], pi32[:, :], 5, None, ALU.arith_shift_right))
            s_v.wait(v)
            s_v.inc(v.tensor_scalar(ib1[:, :], i5[:, :], 1, None, ALU.bitwise_and))
            s_v.wait(v)
            s_v.inc(v.tensor_scalar(iw[:, :], ib1[:, :], -64, 32, ALU.mult, ALU.add))
            s_v.wait(v)
            s_v.inc(v.tensor_tensor(iw[:, :], iw[:, :], pi32[:, :], ALU.add))
            s_v.wait(v)
            s_v.inc(v.tensor_copy(tsw[:, :], iw[:, :]))
            s_v.wait(v)
            s_v.inc(v.tensor_scalar(sgn[:, :], ib1[:, :], 2, -1, ALU.mult, ALU.add))
            s_v.wait(v)
            s_v.inc(v.tensor_scalar(Pm[:, :], TH[:, 0:128], tsw[:, 0:1], None,
                                    ALU.is_equal))
            s_v.wait(v)
            s_v.inc(v.tensor_scalar(iw[:, :], i5[:, :], -32, None, ALU.mult))
            s_v.wait(v)
            s_v.inc(v.tensor_tensor(iw[:, :], iw[:, :], pi32[:, :], ALU.add))
            s_v.wait(v)
            s_v.inc(v.tensor_copy(invf[:, :], iw[:, :]))

        @block.scalar
        def _(sc):
            s_v.wait(sc)
            import math as _m
            s_a.inc(sc.activation(out=invf[:, :], in_=invf[:, :], func=AF.Exp,
                                  scale=-_m.log(10000.0) / 32.0))

        @block.vector
        def _(v):
            s_a.wait(v)
            s_v.inc(v.tensor_scalar(TH[:, :], TH[:, :], invf[:, 0:1], None, ALU.mult))
            s_v.wait(v)
            s_v.inc(v.tensor_scalar(T1[:, :], TH[:, :], 1.0 / (2 * PI), None, ALU.mult))
            s_v.wait(v)
            s_v.inc(v.tensor_copy(TI[:, :], T1[:, :]))
            s_v.wait(v)
            s_v.inc(v.tensor_copy(T1[:, :], TI[:, :]))
            s_v.wait(v)
            s_v.inc(v.scalar_tensor_tensor(T2[:, :], T1[:, :], -2 * PI, TH[:, :],
                                           ALU.mult, ALU.add))

        @block.scalar
        def _(sc):
            s_v.wait(sc)
            s_a.inc(sc.activation(out=C2[:, :], in_=T2[:, :], func=AF.Sin))

        @block.vector
        def _(v):
            s_a.wait(v)
            s_v.inc(v.tensor_scalar(C2[:, :], C2[:, :], sgn[:, 0:1], None, ALU.mult))
            s_v.wait(v)
            s_v.inc(v.tensor_scalar(T1[:, :], TH[:, :], 1.0 / (2 * PI), 1.0 / 4.0,
                                    ALU.mult, ALU.add))
            s_v.wait(v)
            s_v.inc(v.tensor_copy(TI[:, :], T1[:, :]))
            s_v.wait(v)
            s_v.inc(v.tensor_copy(T1[:, :], TI[:, :]))
            s_v.wait(v)
            s_v.inc(v.scalar_tensor_tensor(T2[:, :], T1[:, :], -2 * PI, TH[:, :],
                                           ALU.mult, ALU.add))
            s_v.wait(v)
            s_v.inc(v.tensor_scalar(T2[:, :], T2[:, :], 1.0, PI / 2.0, ALU.mult, ALU.add))

        @block.scalar
        def _(sc):
            s_v.wait(sc)
            s_a.inc(sc.activation(out=C1[:, :], in_=T2[:, :], func=AF.Sin))

        ctx.barrier()
        tbl_scope.close()

        for b in range(NB):
            sc2b = contextlib.ExitStack()
            ps_sw = sc2b.enter_context(nc.psum_tensor(tag + "ps_sw%d" % b, [128, S], F32))
            ps_tr = sc2b.enter_context(nc.psum_tensor(tag + "ps_tr%d" % b, [128, 128], BF16))
            for c in range(FC):
                for which in ("k", "q"):
                    rows0 = 1024 if which == "k" else 0

                    @block.sync
                    def _(sy, c=c, rows0=rows0, b=b):
                        s_v.wait(sy)
                        s_a.wait(sy)
                        s_sp.wait(sy)
                        s_at.inc(sy.dma_start(
                            out=QIN[:, :],
                            in_=qkT_d[rows0 + c * 128:rows0 + (c + 1) * 128,
                                      b * S:(b + 1) * S]), 16)

                    @block.tensor
                    def _(t, c=c):
                        s_at.wait(t)
                        s_v.wait(t)
                        s_g.wait(t)
                        last = None
                        for nq in range(4):
                            last = t.matmul(ps_sw[:, nq * 512:(nq + 1) * 512], Pm[:, :],
                                            QIN[:, nq * 512:(nq + 1) * 512],
                                            start=True, stop=True)
                        s_pe.inc(last)

                    @block.vector
                    def _(v, which=which, c=c):
                        s_pe.wait(v)
                        s_a.wait(v)
                        sc_ = 0.125 if which == "q" else 1.0
                        s_v.inc(v.tensor_mul(T1[:, :], QIN[:, :], C1[:, :]))
                        s_v.wait(v)
                        s_v.inc(v.tensor_mul(T2[:, :], ps_sw[:, :], C2[:, :]))
                        s_v.wait(v)
                        s_v.inc(v.tensor_add(T1[:, :], T1[:, :], T2[:, :]))
                        s_v.wait(v)
                        s_v.inc(v.tensor_scalar(T2[:, :], T1[:, :], 0.0, sc_,
                                                ALU.min, ALU.mult))
                        s_v.wait(v)
                        s_v.inc(v.tensor_scalar(T1[:, :], T1[:, :], 0.0, sc_,
                                                ALU.max, ALU.mult))

                    @block.scalar
                    def _(sc):
                        s_v.wait(sc)
                        s_a.inc(sc.activation(out=T2[:, :], in_=T2[:, :], func=AF.Exp))

                    @block.vector
                    def _(v, which=which, c=c):
                        s_a.wait(v)
                        dst = KFC[:, :] if which == "k" else QF[:, c * S:(c + 1) * S]
                        s_v.inc(v.tensor_add(dst, T1[:, :], T2[:, :]))

                    if which == "k":
                        @block.tensor
                        def _(t, c=c):
                            s_v.wait(t)
                            for j in range(16):
                                if j > 0:
                                    t.wait_ge(s_v.h, s_v.n + j)
                                t.transpose(ps_tr[:, :], KFC[:, j * 128:(j + 1) * 128],
                                            IDN[:, :])
                                s_pe.inc(t.nop())

                        @block.vector
                        def _(v, c=c):
                            base = s_pe.n - 16
                            for j in range(16):
                                v.wait_ge(s_pe.h, base + j + 1)
                                s_v.inc(v.tensor_copy(
                                    KT[:, j * 1024 + c * 128:j * 1024 + (c + 1) * 128],
                                    ps_tr[:, :]))

            ctx.barrier()
            sc2b.close()

            with contextlib.ExitStack() as sc3:
                ps_kv = sc3.enter_context(nc.psum_tensor(tag + "ps_kv%d" % b, [64, 65], F32))
                ps_out = sc3.enter_context(nc.psum_tensor(tag + "ps_out%d" % b, [65, S], F32))
                ps_rb2 = sc3.enter_context(nc.psum_tensor(tag + "ps_rb2%d" % b, [64, 1024], F32))

                for h in range(H):
                    ib = h % 2

                    @block.sync
                    def _(sy, h=h, ib=ib, b=b):
                        s_pe.wait(sy)
                        s_g.wait(sy)
                        s_at.inc(sy.dma_start(
                            out=va[ib][:, :].rearrange("p (kc d) -> p kc d", kc=16)[:, :, 0:64],
                            in_=vtok_d[b * S:(b + 1) * S, h * 64:(h + 1) * 64]
                            .rearrange("(kc p) d -> p kc d", p=128)), 16)

                    @block.tensor
                    def _(t, h=h, ib=ib):
                        s_at.wait(t)
                        s_v.wait(t)
                        last = None
                        for j in range(16):
                            last = t.matmul(ps_kv[:, :],
                                            KT[:, j * 1024 + h * 64:j * 1024 + (h + 1) * 64],
                                            va[ib][:, j * 65:(j + 1) * 65],
                                            start=(j == 0), stop=(j == 15))
                        s_pe.inc(last)

                    @block.vector
                    def _(v):
                        s_pe.wait(v)
                        with nc.allow_low_precision(reason="bf16 matmul operand"):
                            s_v.inc(v.tensor_copy(kvs[:, :], ps_kv[:, :]))

                    if h % 2 == 1:
                        @block.gpsimd
                        def _(g):
                            s_v.wait(g)
                            s_at.inc(g.dma_start(out=kvs_hi[64:128, :], in_=kvs[:, :]), 16)

                    @block.tensor
                    def _(t, h=h):
                        s_v.wait(t)
                        s_at.wait(t)
                        ch, half64 = h // 2, (h % 2) * 64
                        lhs = kvs[:, :] if h % 2 == 0 else kvs_hi[64:128, :]
                        last = None
                        for nq in range(4):
                            last = t.matmul(ps_out[:, nq * 512:(nq + 1) * 512], lhs,
                                            QF[half64:half64 + 64,
                                               ch * S + nq * 512:ch * S + (nq + 1) * 512],
                                            start=True, stop=True)
                        s_pe.inc(last)

                    @block.vector
                    def _(v):
                        s_pe.wait(v)
                        s_v.inc(v.tensor_scalar(recf[0:1, :], ps_out[64:65, :], 1e-6,
                                                None, ALU.max))
                        s_v.wait(v)
                        with nc.allow_low_precision(reason="bf16 matmul operand"):
                            s_v.inc(v.reciprocal(rec_row[:, :], recf[0:1, :]))

                    for half in range(2):
                        @block.tensor
                        def _(t, half=half):
                            s_v.wait(t)
                            if half == 1:
                                s_a.wait(t)
                            t.matmul(ps_rb2[:, 0:512], ones_row[:, 0:64],
                                     rec_row[:, half * 1024:half * 1024 + 512],
                                     start=True, stop=True)
                            s_pe.inc(t.matmul(ps_rb2[:, 512:1024], ones_row[:, 0:64],
                                              rec_row[:, half * 1024 + 512:(half + 1) * 1024],
                                              start=True, stop=True))

                        @block.scalar
                        def _(sc, half=half):
                            s_pe.wait(sc)
                            s_v.wait(sc)
                            s_a.inc(sc.copy(out=rb_sb[:, :], in_=ps_rb2[:, :]))

                        @block.vector
                        def _(v, h=h, half=half):
                            s_a.wait(v)
                            ch, half64 = h // 2, (h % 2) * 64
                            s_v.inc(v.tensor_mul(
                                outn[half64:half64 + 64,
                                     ch * S + half * 1024:ch * S + (half + 1) * 1024],
                                ps_out[0:64, half * 1024:(half + 1) * 1024], rb_sb[:, :]))

            ctx.barrier()

            with contextlib.ExitStack() as sc4:
                XLD = ctx.sbuf(tag + "xld%d" % b, [128, S], F32, scope=sc4)
                OST = ctx.sbuf(tag + "ost%d" % b, [128, S], F32, scope=sc4)
                ps_o = [sc4.enter_context(nc.psum_tensor(tag + "ps_o%d_%d" % (b, i),
                                                          [128, 512], F32))
                        for i in range(2)]

                for ms in range(8):
                    @block.sync
                    def _(sy, ms=ms, b=b):
                        s_v.wait(sy)
                        s_g.inc(sy.dma_start(
                            out=XLD[:, :].rearrange("p (j n) -> p j n", n=256),
                            in_=xrow(ms * 128, b * S, S)), 16)

                    @block.tensor
                    def _(t, ms=ms):
                        s_v.wait(t)
                        if ms == 0:
                            s_ld.wait(t)
                        s_a.wait(t)
                        s_sp.wait(t)
                        for q4 in range(4):
                            ps = ps_o[q4 % 2]
                            if q4 >= 2:
                                t.wait_ge(s_v.h, s_v.n + q4 - 1)
                            last = None
                            for kc in range(FC):
                                last = t.matmul(
                                    ps[:, :],
                                    WOUT[:, kc * 1024 + ms * 128:kc * 1024 + (ms + 1) * 128],
                                    outn[:, kc * S + q4 * 512:kc * S + (q4 + 1) * 512],
                                    start=(kc == 0), stop=(kc == FC - 1))
                            s_pe.inc(last)

                    @block.vector
                    def _(v, ms=ms):
                        base = s_pe.n - 4
                        s_g.wait(v)
                        s_sp.wait(v)
                        for q4 in range(4):
                            ps = ps_o[q4 % 2]
                            v.wait_ge(s_pe.h, base + q4 + 1)
                            s_v.inc(v.tensor_add(OST[:, q4 * 512:(q4 + 1) * 512],
                                                 XLD[:, q4 * 512:(q4 + 1) * 512], ps[:, :]))

                    @block.scalar
                    def _(sc, ms=ms, b=b):
                        s_v.wait(sc)
                        s_sp.inc(sc.dma_start(
                            out=yout_d[ms * 128:(ms + 1) * 128, b * S:(b + 1) * S],
                            in_=OST[:, :]), 16)

            ctx.barrier()


def _run_l1_attn(x_t, w_qkv, w_out):
    nc = bass.Bass("TRN2", target_bir_lowering=False, debug=False)
    xin_h = nc.dram_tensor("xT", [D, NT], F32, kind="ExternalInput")
    wq_d = nc.dram_tensor("wq", [D, 3072], F32, kind="ExternalInput").ap()
    wo_d = nc.dram_tensor("wo", [D, D], F32, kind="ExternalInput").ap()
    yout = nc.dram_tensor("yT", [D, NT], F32, kind="ExternalOutput").ap()
    qkT_d = nc.dram_tensor("qkT", [2048, NT], BF16).ap()
    vtok_d = nc.dram_tensor("vtok", [NT, D], BF16).ap()

    with contextlib.ExitStack() as es:
        block = es.enter_context(nc.Block())
        ctx = Ctx(nc, block, es)
        xprov, xrow, _ = _prov_flat(xin_h)
        emit_l1_attn(ctx, xprov, xrow, yout, wq_d, wo_d, qkT_d, vtok_d)
        ctx.barrier(engines=("sync",))

    wqT = np.ascontiguousarray(np.asarray(w_qkv, np.float32).T)
    woT = np.ascontiguousarray(np.asarray(w_out, np.float32).T)
    in_maps = [{"xT": x_t[i], "wq": wqT, "wo": woT} for i in range(8)]
    res = run_bass_kernel_spmd(nc, in_maps, core_ids=list(range(8)))
    return np.stack([np.asarray(r["yT"]) for r in res.results])



def emit_l1_ffn(ctx, xprov, xrow, yout_d, wr_d, br_d, w1_d, b1_d, w2_d, b2_d, lnx_d,
                tag="m1"):
    nc, block = ctx.nc, ctx.block
    s_ld = ctx.sem(tag + "ld")
    s_w = ctx.sem(tag + "wl")
    s_sp = ctx.sem(tag + "sp")
    s_pe = ctx.sem(tag + "pe")
    s_v = ctx.sem(tag + "vv")
    s_a = ctx.sem(tag + "aa")
    s_g = ctx.sem(tag + "gg")

    ones2 = ctx.sbuf(tag + "ones2", [128, 2], BF16)
    onesf = ctx.sbuf(tag + "onesf", [128, 1], F32)
    ones_row = ctx.sbuf(tag + "onesr", [1, 128], BF16)

    @block.gpsimd
    def _(g):
        g.memset(ones2[:, 0:1], -1.0 / D)
        g.memset(ones2[:, 1:2], 1.0)
        g.memset(onesf[:, 0:1], -1.0 / D)
        s_g.inc(g.memset(ones_row[:, :], 1.0))

    with contextlib.ExitStack() as sca:
        XRC = ctx.sbuf(tag + "xrc", [128, FC * TCH], F32, scope=sca)
        XC = ctx.sbuf(tag + "xc", [128, FC * TCH], F32, scope=sca)
        LNB = ctx.sbuf(tag + "lnb", [128, FC * TCH], BF16, scope=sca)
        row1 = ctx.sbuf(tag + "row1", [1, TCH], BF16, scope=sca)
        rowr = ctx.sbuf(tag + "rowr", [1, TCH], BF16, scope=sca)
        rowv = ctx.sbuf(tag + "rowv", [1, TCH], F32, scope=sca)
        ps_sum = sca.enter_context(nc.psum_tensor(tag + "ps_sum", [1, TCH], F32))
        ps_mb = sca.enter_context(nc.psum_tensor(tag + "ps_mb", [128, TCH], F32))
        ps_rb = sca.enter_context(nc.psum_tensor(tag + "ps_rb", [128, TCH], F32))
        consts = (onesf, ones2, ones_row, ps_sum, ps_mb, ps_rb, row1, rowr, rowv)

        for tc in range(NTC):
            c0 = tc * TCH

            @block.sync
            def _(sy, c0=c0):
                s_a.wait(sy)
                s_v.wait(sy)
                s_pe.wait(sy)
                s_sp.wait(sy)
                s_g.inc(sy.dma_start(
                    out=XRC[:, :].rearrange("p (kc n) -> p kc n", kc=FC),
                    in_=xprov(c0, TCH)), 16)

            @block.tensor
            def _(t):
                s_g.wait(t)

            @block.vector
            def _(v):
                s_g.wait(v)

            @block.scalar
            def _(sc):
                s_g.wait(sc)

            emit_ln(ctx, XRC, LNB, XC, consts, 0, (s_pe, s_v, s_a), xstride=TCH)

            @block.sync
            def _(sy, c0=c0):
                s_v.wait(sy)
                s_sp.inc(sy.dma_start(
                    out=lnx_d.rearrange("(kc p) n -> p kc n", p=128)[:, :, c0:c0 + TCH],
                    in_=LNB[:, :].rearrange("p (kc n) -> p kc n", kc=FC)), 16)

        ctx.barrier()

    comb = ctx.sbuf(tag + "comb", [128, 32 * 16], F32)
    combT = ctx.sbuf(tag + "combT", [16, NT], BF16)
    SEL = ctx.sbuf(tag + "sel", [16, 16 * 128], BF16)

    with contextlib.ExitStack() as scb:
        WR = ctx.sbuf(tag + "wrr", [128, FC * 16], BF16, scope=scb)
        brb = ctx.sbuf(tag + "brb", [128, 16], F32, scope=scb)
        brrow = ctx.sbuf(tag + "brrow", [1, 16], BF16, scope=scb)
        LNL = ctx.sbuf(tag + "lnl", [128, FC * 128], BF16, scope=scb)
        lg = ctx.sbuf(tag + "lg", [128, 16], F32, scope=scb)
        pr2 = ctx.sbuf(tag + "pr2", [128, 16], F32, scope=scb)
        msk1 = ctx.sbuf(tag + "msk1", [128, 16], F32, scope=scb)
        msk2 = ctx.sbuf(tag + "msk2", [128, 16], F32, scope=scb)
        m1c = ctx.sbuf(tag + "m1c", [128, 1], F32, scope=scb)
        m2c = ctx.sbuf(tag + "m2c", [128, 1], F32, scope=scb)
        rcc = ctx.sbuf(tag + "rcc", [128, 1], F32, scope=scb)
        selio = ctx.sbuf(tag + "selio", [16, 16 * 128], F32, scope=scb)
        pi16 = ctx.sbuf(tag + "pi16", [16, 1], F32, scope=scb)
        ps_r = scb.enter_context(nc.psum_tensor(tag + "ps_r", [128, 16], F32))
        ps_t = scb.enter_context(nc.psum_tensor(tag + "ps_t", [16, 128], BF16))
        cmb_bf = ctx.sbuf(tag + "cmbbf", [128, 16], BF16, scope=scb)
        idn2 = ctx.sbuf(tag + "idn2", [128, 128], BF16, scope=scb)

        @block.gpsimd
        def _(g):
            s_ld.inc(g.dma_start(
                out=WR[:, :].rearrange("p (kc m) -> p kc m", kc=FC),
                in_=wr_d.rearrange("(kc p) m -> p kc m", p=128)), 16)
            s_ld.inc(g.dma_start(out=brrow[:, :], in_=br_d[0:1, :]), 16)
            g.iota(selio[:, :].rearrange("p (e n) -> p e n", e=16), [[1, 16], [0, 128]],
                   base=0, channel_multiplier=0, allow_small_or_imprecise_dtypes=True)
            g.memset(idn2[:, :], 0.0)
            g.affine_select(out=idn2[:, :], in_=idn2[:, :], compare_op=ALU.not_equal,
                            fill=1.0, base=0, pattern=[[-1, 128]], channel_multiplier=1)
            s_g.inc(g.iota(pi16[:, :], [[1, 1]], base=0, channel_multiplier=1,
                           allow_small_or_imprecise_dtypes=True))

        @block.vector
        def _(v):
            s_g.wait(v)
            s_v.inc(v.tensor_scalar(SEL[:, :], selio[:, :], pi16[:, 0:1], None,
                                    ALU.is_equal))

        @block.tensor
        def _(t):
            s_ld.wait(t)
            s_g.wait(t)
            s_pe.inc(t.matmul(ps_r[:, :], ones_row[:, :], brrow[:, :],
                              start=True, stop=True))

        @block.vector
        def _(v):
            s_pe.wait(v)
            s_v.inc(v.tensor_copy(brb[:, :], ps_r[:, :]))

        for ts in range(32):
            @block.sync
            def _(sy, ts=ts):
                s_pe.wait(sy)
                s_v.wait(sy)
                s_g.inc(sy.dma_start(
                    out=LNL[:, :].rearrange("p (kc n) -> p kc n", kc=FC),
                    in_=lnx_d.rearrange("(kc p) n -> p kc n", p=128)
                    [:, :, ts * 128:(ts + 1) * 128]), 16)

            @block.tensor
            def _(t, ts=ts):
                s_g.wait(t)
                s_v.wait(t)
                last = None
                for kc in range(FC):
                    last = t.matmul(ps_r[:, :], LNL[:, kc * 128:(kc + 1) * 128],
                                    WR[:, kc * 16:(kc + 1) * 16],
                                    start=(kc == 0), stop=(kc == FC - 1))
                s_pe.inc(last)

            @block.vector
            def _(v, ts=ts):
                s_pe.wait(v)
                s_a.wait(v)
                s_v.inc(v.tensor_add(lg[:, :], ps_r[:, :], brb[:, :]))
                s_v.wait(v)
                s_v.inc(v.reduce_max(m1c[:, :], lg[:, :], axis=AX.X))
                s_v.wait(v)
                s_v.inc(v.tensor_scalar(m2c[:, :], m1c[:, :], -1.0, None, ALU.mult))

            @block.scalar
            def _(sc):
                s_v.wait(sc)
                s_a.inc(sc.activation(out=lg[:, :], in_=lg[:, :], func=AF.Exp,
                                      bias=m2c[:, 0:1]))

            @block.vector
            def _(v, ts=ts):
                s_a.wait(v)
                s_v.inc(v.reduce_sum(m1c[:, :], lg[:, :], axis=AX.X))
                s_v.wait(v)
                s_v.inc(v.reciprocal(rcc[:, :], m1c[:, :]))
                s_v.wait(v)
                s_v.inc(v.tensor_scalar(lg[:, :], lg[:, :], rcc[:, 0:1], None, ALU.mult))
                s_v.wait(v)
                s_v.inc(v.reduce_max(m1c[:, :], lg[:, :], axis=AX.X))
                s_v.wait(v)
                s_v.inc(v.tensor_scalar(msk1[:, :], lg[:, :], m1c[:, 0:1], None,
                                        ALU.is_equal))
                s_v.wait(v)
                s_v.inc(v.scalar_tensor_tensor(pr2[:, :], msk1[:, :], -4.0, lg[:, :],
                                               ALU.mult, ALU.add))
                s_v.wait(v)
                s_v.inc(v.reduce_max(m2c[:, :], pr2[:, :], axis=AX.X))
                s_v.wait(v)
                s_v.inc(v.tensor_scalar(msk2[:, :], pr2[:, :], m2c[:, 0:1], None,
                                        ALU.is_equal))
                s_v.wait(v)
                s_v.inc(v.tensor_add(rcc[:, :], m1c[:, :], m2c[:, :]))
                s_v.wait(v)
                s_v.inc(v.tensor_scalar(rcc[:, :], rcc[:, :], 1e-9, None, ALU.max))
                s_v.wait(v)
                s_v.inc(v.reciprocal(rcc[:, :], rcc[:, :]))
                s_v.wait(v)
                s_v.inc(v.tensor_scalar(msk1[:, :], msk1[:, :], m1c[:, 0:1], None,
                                        ALU.mult))
                s_v.wait(v)
                s_v.inc(v.tensor_scalar(msk2[:, :], msk2[:, :], m2c[:, 0:1], None,
                                        ALU.mult))
                s_v.wait(v)
                s_v.inc(v.tensor_add(msk1[:, :], msk1[:, :], msk2[:, :]))
                s_v.wait(v)
                s_v.inc(v.tensor_scalar(comb[:, ts * 16:(ts + 1) * 16], msk1[:, :],
                                        rcc[:, 0:1], None, ALU.mult))
                s_v.wait(v)
                with nc.allow_low_precision(reason="bf16 matmul operand"):
                    s_v.inc(v.tensor_copy(cmb_bf[:, :], comb[:, ts * 16:(ts + 1) * 16]))

            @block.tensor
            def _(t, ts=ts):
                s_v.wait(t)
                s_g.wait(t)
                s_pe.inc(t.transpose(ps_t[:, 0:128], cmb_bf[:, :], idn2[:, :]))

            @block.vector
            def _(v, ts=ts):
                s_pe.wait(v)
                s_v.inc(v.tensor_copy(combT[:, ts * 128:(ts + 1) * 128], ps_t[0:16, 0:128]))

        ctx.barrier()

    with contextlib.ExitStack() as scc:
        W1B = [ctx.sbuf(tag + "w1b%d" % i, [128, FC * 2048], BF16, scope=scc)
               for i in range(2)]
        W2B = ctx.sbuf(tag + "w2b", [128, 16 * 1024], BF16, scope=scc)
        ACC = ctx.sbuf(tag + "acc", [128, FC * 1024], F32, scope=scc)
        LNX = ctx.sbuf(tag + "lnx3", [128, 2 * FC * 512], BF16, scope=scc)
        HSB = ctx.sbuf(tag + "hsb", [128, 16 * 512], BF16, scope=scc)
        GTMP = ctx.sbuf(tag + "gtmp", [128, 2 * 512], F32, scope=scc)
        B2S = ctx.sbuf(tag + "b2s", [16, 1024], BF16, scope=scc)
        b1c = ctx.sbuf(tag + "b1c", [128, 256], F32, scope=scc)
        XLD = ctx.sbuf(tag + "xld", [128, 512], F32, scope=scc)
        LNL2 = ctx.sbuf(tag + "lnl2", [128, 512], BF16, scope=scc)
        OST = ctx.sbuf(tag + "ost", [128, 512], F32, scope=scc)

        ps_h = [scc.enter_context(nc.psum_tensor(tag + "ps_h%d" % i, [128, 512], F32))
                for i in range(2)]
        ps_m = [scc.enter_context(nc.psum_tensor(tag + "ps_m%d" % i, [128, 512], F32))
                for i in range(2)]
        ps_cb = scc.enter_context(nc.psum_tensor(tag + "ps_cb", [128, 512], F32))

        @block.gpsimd
        def _(g):
            s_ld.inc(g.dma_start(out=B2S[:, :], in_=b2_d[:, :]), 16)
            s_ld.inc(g.dma_start(out=b1c[:, :], in_=b1_d[:, :]), 16)

        def load_w1(g, e, buf):
            s_w.inc(g.dma_start(
                out=W1B[buf][:, :].rearrange("p (kc m) -> p kc m", kc=FC),
                in_=w1_d[e].rearrange("(kc p) m -> p kc m", p=128)), 16)

        def load_w2(g, e):
            s_w.inc(g.dma_start(
                out=W2B[:, :].rearrange("p (hc m) -> p hc m", hc=16),
                in_=w2_d[e].rearrange("(hc p) m -> p hc m", p=128)), 16)

        NPASS = 4
        for half in range(NPASS):
            hc0 = half * 1024

            for tc in range(2):
                cbase = hc0 + tc * 512

                base_v0 = s_v.n

                @block.tensor
                def _(t, tc=tc, cbase=cbase, half=half, base_v0=base_v0):
                    s_v.wait(t)
                    s_ld.wait(t)
                    for os in range(8):
                        ps = ps_m[os % 2]
                        if os >= 2:
                            t.wait_ge(s_v.h, base_v0 + os - 1)
                        s_pe.inc(t.matmul(ps[:, :], B2S[:, os * 128:(os + 1) * 128],
                                          combT[:, cbase:cbase + 512],
                                          start=True, stop=True))

                @block.vector
                def _(v, tc=tc):
                    base = s_pe.n - 8
                    for os in range(8):
                        ps = ps_m[os % 2]
                        v.wait_ge(s_pe.h, base + os + 1)
                        s_v.inc(v.tensor_copy(ACC[:, os * 1024 + tc * 512:os * 1024 + (tc + 1) * 512],
                                              ps[:, :]))

            nld = [0]

            def lnx_load(it, half=half):
                e2, tc2 = divmod(it, 2)
                cb2 = half * 1024 + tc2 * 512

                @block.sync
                def _(sy, it=it, cb2=cb2):
                    s_pe.wait(sy)
                    sl0 = (it % 2) * FC * 512
                    s_g.inc(sy.dma_start(
                        out=LNX[:, sl0:sl0 + FC * 512].rearrange(
                            "p (kc n) -> p kc n", kc=FC),
                        in_=lnx_d.rearrange("(kc p) n -> p kc n", p=128)
                        [:, :, cb2:cb2 + 512]), 16)
                nld[0] += 1

            lnx_base_g = s_g.n
            lnx_load(0)

            for e in range(16):
                @block.gpsimd
                def _(g, e=e, half=half):
                    if e == 0 and half == 0:
                        load_w1(g, 0, 0)
                        load_w2(g, 0)
                    s_pe.wait(g)
                    nxt = e + 1 + half * 16
                    if nxt < 16 * 4:
                        load_w1(g, (e + 1) % 16, (e + 1) % 2)

                @block.gpsimd
                def _(g, e=e, half=half):
                    if e > 0 or half > 0:
                        s_pe.wait(g)
                        load_w2(g, e)

                for tc in range(2):
                    cbase = hc0 + tc * 512
                    it = e * 2 + tc
                    if it + 1 < 32:
                        lnx_load(it + 1)
                    lslot = (it % 2) * FC * 512

                    @block.tensor
                    def _(t, tc=tc, cbase=cbase, e=e, it=it, lslot=lslot):
                        t.wait_ge(s_g.h, lnx_base_g + 16 * (it + 1))
                        s_w.wait(t)
                        s_v.wait(t)
                        t.matmul(ps_cb[:, :], SEL[:, e * 128:(e + 1) * 128],
                                 combT[:, cbase:cbase + 512], start=True, stop=True)
                        s_pe.inc(t.nop())
                        base_a = s_a.n
                        for hs in range(16):
                            ps = ps_h[hs % 2]
                            if hs >= 2:
                                t.wait_ge(s_a.h, base_a + hs - 1)
                            last = None
                            for kc in range(FC):
                                last = t.matmul(
                                    ps[:, :],
                                    W1B[e % 2][:, kc * 2048 + hs * 128:kc * 2048 + (hs + 1) * 128],
                                    LNX[:, lslot + kc * 512:lslot + (kc + 1) * 512],
                                    start=(kc == 0), stop=(kc == FC - 1))
                            s_pe.inc(last)

                    @block.scalar
                    def _(sc, e=e):
                        base = s_pe.n - 16
                        base_v = s_v.n
                        for hs in range(16):
                            ps = ps_h[hs % 2]
                            sc.wait_ge(s_pe.h, base + hs + 1)
                            if hs > 1:
                                sc.wait_ge(s_v.h, base_v + hs - 1)
                            s_a.inc(sc.activation(
                                out=GTMP[:, (hs % 2) * 512:(hs % 2) * 512 + 512],
                                in_=ps[:, :], func=AF.Gelu,
                                bias=b1c[:, e * 16 + hs:e * 16 + hs + 1]))

                    @block.vector
                    def _(v, e=e):
                        base = s_a.n - 16
                        for hs in range(16):
                            v.wait_ge(s_a.h, base + hs + 1)
                            s_v.inc(v.tensor_mul(
                                HSB[:, hs * 512:(hs + 1) * 512],
                                GTMP[:, (hs % 2) * 512:(hs % 2) * 512 + 512],
                                ps_cb[:, :]))

                    @block.tensor
                    def _(t, tc=tc, e=e):
                        base_mul = s_v.n - 16
                        for os in range(8):
                            ps = ps_m[os % 2]
                            if os == 0:
                                pass
                            elif os == 1:
                                t.wait_ge(s_v.h, base_mul + 16)
                            elif os >= 2:
                                t.wait_ge(s_v.h, s_v.n + os - 1)
                            last = None
                            for hc in range(16):
                                if os == 0:
                                    t.wait_ge(s_v.h, base_mul + hc + 1)
                                last = t.matmul(
                                    ps[:, :],
                                    W2B[:, hc * 1024 + os * 128:hc * 1024 + (os + 1) * 128],
                                    HSB[:, hc * 512:(hc + 1) * 512],
                                    start=(hc == 0), stop=(hc == 15))
                            s_pe.inc(last)

                    @block.vector
                    def _(v, tc=tc):
                        base = s_pe.n - 8
                        for os in range(8):
                            ps = ps_m[os % 2]
                            v.wait_ge(s_pe.h, base + os + 1)
                            sl = ACC[:, os * 1024 + tc * 512:os * 1024 + (tc + 1) * 512]
                            s_v.inc(v.tensor_add(sl, sl, ps[:, :]))

            for os in range(8):
                for tc in range(2):
                    cbase = hc0 + tc * 512

                    @block.sync
                    def _(sy, os=os, tc=tc, cbase=cbase):
                        s_v.wait(sy)
                        s_sp.wait(sy)
                        s_g.inc(sy.dma_start(out=XLD[:, :].rearrange("p (j n) -> p j n", n=256),
                                             in_=xrow(os * 128, cbase, 512)), 16)
                        s_g.inc(sy.dma_start(out=LNL2[:, :],
                                             in_=lnx_d[os * 128:(os + 1) * 128,
                                                       cbase:cbase + 512]), 16)

                    @block.vector
                    def _(v, os=os, tc=tc):
                        s_g.wait(v)
                        s_v.inc(v.tensor_add(OST[:, :], XLD[:, :], LNL2[:, :]))
                        s_v.wait(v)
                        s_v.inc(v.tensor_add(OST[:, :], OST[:, :],
                                             ACC[:, os * 1024 + tc * 512:os * 1024 + (tc + 1) * 512]))

                    @block.scalar
                    def _(sc, os=os, cbase=cbase):
                        s_v.wait(sc)
                        s_sp.inc(sc.dma_start(
                            out=yout_d[os * 128:(os + 1) * 128, cbase:cbase + 512],
                            in_=OST[:, :]), 16)

            ctx.barrier()


def _run_l1_ffn(x_t, Wr, br, W1, b1, W2, b2):
    nc = bass.Bass("TRN2", target_bir_lowering=False, debug=False)
    xin_h = nc.dram_tensor("xT", [D, NT], F32, kind="ExternalInput")
    wr_d = nc.dram_tensor("wr", [D, 16], F32, kind="ExternalInput").ap()
    br_d = nc.dram_tensor("br", [1, 16], F32, kind="ExternalInput").ap()
    w1_d = nc.dram_tensor("w1", [16, D, 2048], F32, kind="ExternalInput").ap()
    b1_d = nc.dram_tensor("b1", [128, 256], F32, kind="ExternalInput").ap()
    w2_d = nc.dram_tensor("w2", [16, 2048, D], F32, kind="ExternalInput").ap()
    b2_d = nc.dram_tensor("b2", [16, D], F32, kind="ExternalInput").ap()
    yout = nc.dram_tensor("yT", [D, NT], F32, kind="ExternalOutput").ap()
    lnx_d = nc.dram_tensor("lnx", [D, NT], BF16).ap()

    with contextlib.ExitStack() as es:
        block = es.enter_context(nc.Block())
        ctx = Ctx(nc, block, es)
        xprov, xrow, _ = _prov_flat(xin_h)
        emit_l1_ffn(ctx, xprov, xrow, yout, wr_d, br_d, w1_d, b1_d, w2_d, b2_d, lnx_d)
        ctx.barrier(engines=("sync",))

    b1c = np.ascontiguousarray(np.asarray(b1, np.float32).reshape(16, 16, 128)
                               .transpose(2, 0, 1).reshape(128, 256))
    in_maps = [{"xT": x_t[i], "wr": np.asarray(Wr, np.float32),
                "br": np.asarray(br, np.float32).reshape(1, 16), "w1": W1, "b1": b1c,
                "w2": W2, "b2": b2} for i in range(8)]
    res = run_bass_kernel_spmd(nc, in_maps, core_ids=list(range(8)))
    return np.stack([np.asarray(r["yT"]) for r in res.results])



def _run_l0_attn(x_t, wq, bq, wo, bo):
    nc = bass.Bass("TRN2", target_bir_lowering=False, debug=False)
    xin = nc.dram_tensor("xT", [D, NT], F32, kind="ExternalInput").ap()
    wq_d = nc.dram_tensor("wq", [D, 3072], F32, kind="ExternalInput").ap()
    bq_d = nc.dram_tensor("bq", [128, 24], F32, kind="ExternalInput").ap()
    wo_d = nc.dram_tensor("wo", [D, D], F32, kind="ExternalInput").ap()
    bo_d = nc.dram_tensor("bo", [128, 8], F32, kind="ExternalInput").ap()
    yout = nc.dram_tensor("yT", [D, NT], F32, kind="ExternalOutput").ap()
    qkT_d = nc.dram_tensor("qkT", [2048, NT], BF16).ap()
    vtok_d = nc.dram_tensor("vtok", [NT, D], BF16).ap()

    with contextlib.ExitStack() as es:
        block = es.enter_context(nc.Block())
        ctx = Ctx(nc, block, es)
        XR = ctx.sbuf("XR", [128, FC * NT], F32)
        s_x = ctx.sem("zx")
        s_out = ctx.sem("zo")

        @block.sync
        def _(sy):
            s_x.inc(sy.dma_start(out=XR[:, :].rearrange("p (kc n) -> p kc n", kc=FC),
                                 in_=xin.rearrange("(kc p) n -> p kc n", p=128)), 16)

        for en in ("tensor", "vector", "scalar", "gpsimd"):
            def mk(eng):
                s_x.wait(eng)
            getattr(block, en)(mk)

        emit_l0_attn(ctx, XR, wq_d, bq_d, wo_d, bo_d, qkT_d, vtok_d)

        ctx.barrier(engines=("sync",))

        @block.sync
        def _(sy):
            s_out.inc(sy.dma_start(out=yout.rearrange("(kc p) n -> p kc n", p=128),
                                   in_=XR[:, :].rearrange("p (kc n) -> p kc n", kc=FC)), 16)
            s_out.wait(sy)

    bq2 = np.ascontiguousarray(bq.reshape(24, 128).T)
    bo2 = np.ascontiguousarray(bo.reshape(8, 128).T)
    in_maps = [{"xT": x_t[i], "wq": wq, "bq": bq2, "wo": wo, "bo": bo2}
               for i in range(8)]
    res = run_bass_kernel_spmd(nc, in_maps, core_ids=list(range(8)))
    return np.stack([np.asarray(r["yT"]) for r in res.results])


def _to_fm(x):
    return np.ascontiguousarray(
        x.reshape(8, NB, S, D).transpose(0, 3, 1, 2).reshape(8, D, NT))


def _from_fm(y_t):
    return np.ascontiguousarray(
        y_t.reshape(8, D, NB, S).transpose(0, 2, 3, 1).reshape(16, S, D))


def _run_l1_both(x_t, ins):
    nc = bass.Bass("TRN2", target_bir_lowering=False, debug=False)
    F = lambda nm, sh: nc.dram_tensor(nm, sh, F32, kind="ExternalInput").ap()
    xin_h = nc.dram_tensor("xT", [D, NT], F32, kind="ExternalInput")
    b_wq = F("b_wq", [D, 3072])
    b_wo = F("b_wo", [D, D])
    m_wr = F("m_wr", [D, 16])
    m_br = F("m_br", [1, 16])
    m_w1 = F("m_w1", [16, D, 2048])
    m_b1 = F("m_b1", [128, 256])
    m_w2 = F("m_w2", [16, 2048, D])
    m_b2 = F("m_b2", [16, D])
    yout = nc.dram_tensor("yT", [D, NT], F32, kind="ExternalOutput").ap()

    qkT1 = nc.dram_tensor("qkT1", [2048, NT], BF16).ap()
    vtok1 = nc.dram_tensor("vtok1", [NT, D], BF16).ap()
    lnxm = nc.dram_tensor("lnxm", [D, NT], BF16).ap()
    y3 = nc.dram_tensor("y3", [D, NT], F32)

    with contextlib.ExitStack() as es:
        block = es.enter_context(nc.Block())
        ctx = Ctx(nc, block, es)
        xprov, xrow, _ = _prov_flat(xin_h)
        emit_l1_attn(ctx, xprov, xrow, y3.ap(), b_wq, b_wo, qkT1, vtok1)
        ctx.barrier()
        xp_f, xr_f, _ = _prov_flat(y3)
        emit_l1_ffn(ctx, xp_f, xr_f, yout, m_wr, m_br, m_w1, m_b1, m_w2, m_b2, lnxm)
        ctx.barrier(engines=("sync",))

    tcol = lambda a: np.ascontiguousarray(np.asarray(a, np.float32).T)
    f32 = lambda k: np.asarray(ins[k], np.float32)
    b1c = np.ascontiguousarray(f32("L1_ffn_b1").reshape(16, 16, 128)
                               .transpose(2, 0, 1).reshape(128, 256))
    base = {"b_wq": tcol(ins["L1_attn_w_qkv"]), "b_wo": tcol(ins["L1_attn_w_out"]),
            "m_wr": f32("L1_ffn_Wr"), "m_br": f32("L1_ffn_br").reshape(1, 16),
            "m_w1": f32("L1_ffn_W1"), "m_b1": b1c, "m_w2": f32("L1_ffn_W2"),
            "m_b2": f32("L1_ffn_b2")}
    in_maps = [dict(base, xT=x_t[i]) for i in range(8)]
    res = run_bass_kernel_spmd(nc, in_maps, core_ids=list(range(8)))
    return np.stack([np.asarray(r["yT"]) for r in res.results])


def _bs_to_ss(y):
    return np.ascontiguousarray(
        y.reshape(8, D, 2, 8, 256).transpose(3, 1, 0, 2, 4).reshape(8, D, NT))


def _ss_to_bs(y):
    return np.ascontiguousarray(
        y.reshape(8, D, 8, 2, 256).transpose(2, 1, 3, 0, 4).reshape(8, D, NT))


def kernel(**inputs):
    f32 = lambda k: np.asarray(inputs[k], np.float32)
    x_t = _to_fm(f32("L0_attn_x"))
    y1 = _run_l0_attn(x_t, f32("L0_attn_Wqkv"), f32("L0_attn_bqkv"),
                      f32("L0_attn_Wout"), f32("L0_attn_bout"))
    y2 = _run_l0_ffn(_bs_to_ss(y1),
                     f32("L0_ffn_wf"), f32("L0_ffn_uf"), f32("L0_ffn_bf"),
                     f32("L0_ffn_wi"), f32("L0_ffn_ui"), f32("L0_ffn_bi"),
                     f32("L0_ffn_wc"), f32("L0_ffn_uc"), f32("L0_ffn_bc"),
                     f32("L0_ffn_wo"), f32("L0_ffn_uo"), f32("L0_ffn_bo"),
                     f32("L0_ffn_in_c"))
    y4 = _run_l1_both(_ss_to_bs(y2), inputs)
    return _from_fm(y4)


def kernel_unfused(**inputs):
    f32 = lambda k: np.asarray(inputs[k], np.float32)
    x_t = _to_fm(f32("L0_attn_x"))
    y1 = _run_l0_attn(x_t, f32("L0_attn_Wqkv"), f32("L0_attn_bqkv"),
                      f32("L0_attn_Wout"), f32("L0_attn_bout"))
    y2 = _run_l0_ffn(_bs_to_ss(y1),
                     f32("L0_ffn_wf"), f32("L0_ffn_uf"), f32("L0_ffn_bf"),
                     f32("L0_ffn_wi"), f32("L0_ffn_ui"), f32("L0_ffn_bi"),
                     f32("L0_ffn_wc"), f32("L0_ffn_uc"), f32("L0_ffn_bc"),
                     f32("L0_ffn_wo"), f32("L0_ffn_uo"), f32("L0_ffn_bo"),
                     f32("L0_ffn_in_c"))
    y3 = _run_l1_attn(_ss_to_bs(y2), f32("L1_attn_w_qkv"), f32("L1_attn_w_out"))
    y4 = _run_l1_ffn(y3, f32("L1_ffn_Wr"), f32("L1_ffn_br"), f32("L1_ffn_W1"),
                     f32("L1_ffn_b1"), f32("L1_ffn_W2"), f32("L1_ffn_b2"))
    return _from_fm(y4)
